# version 16
# baseline (speedup 1.0000x reference)
import contextlib
import numpy as np
import concourse.bass as bass
import concourse.mybir as mybir
from concourse.bass_utils import run_bass_kernel_spmd

F32 = mybir.dt.float32
BF16 = mybir.dt.bfloat16
Alu = mybir.AluOpType
Act = mybir.ActivationFunctionType
AX = mybir.AxisListType.X
P = 128
MAGIC = 12582912.0

FULL_CFG = dict(D=1024, NB=4096, H=16, F=4096)

ENGS = ("sync", "scalar", "vector", "tensor", "gpsimd")


class KB:

    STEP_LIMIT = None

    def __init__(self, nc):
        self.nc = nc
        self.steps = {e: [] for e in ENGS}
        self.counts = {}
        self.sem_handles = {}
        self.seq = 0

    def step(self, eng, fn, wait=(), sig=None, sig_n=1):
        self.seq += 1
        if KB.STEP_LIMIT is not None and self.seq > KB.STEP_LIMIT:
            if sig is not None:
                v = self.counts.get(sig, 0) + sig_n
                self.counts[sig] = v
                return (sig, v)
            return None
        w = tuple(x for x in wait if x is not None)
        self.steps[eng].append((fn, w, sig, sig_n))
        if sig is not None:
            v = self.counts.get(sig, 0) + sig_n
            self.counts[sig] = v
            return (sig, v)
        return None

    def mark(self, sig):
        return (sig, self.counts.get(sig, 0))

    def chain(self, eng, fns, wait=(), sig=None):
        if sig is None:
            sig = {"vector": "V", "scalar": "S", "tensor": "T", "gpsimd": "G"}[eng]
        ev = None
        for i, f in enumerate(fns):
            w = (list(wait) if i == 0 else []) + ([ev] if ev is not None else [])
            ev = self.step(eng, f, wait=w, sig=sig)
        return ev

    def emit(self, block):
        waited = {e: {} for e in ENGS}

        def mk(eng):
            def body(engine):
                for fn, waits, sig, sig_n in self.steps[eng]:
                    for (s, v) in waits:
                        if v > 0 and waited[eng].get(s, 0) < v:
                            engine.wait_ge(self.sem_handles[s], v)
                            waited[eng][s] = v
                    inst = fn(engine)
                    if sig is not None:
                        assert inst is not None, f"{eng} step signalling {sig} returned None"
                        inst.then_inc(self.sem_handles[sig], sig_n)
            return body

        for eng in ENGS:
            getattr(block, eng)(mk(eng))


class Arena:
    def __init__(self, cols):
        self.cols = cols
        self.off = 0
        self.saved = []
        self.hw = 0

    def alloc(self, n):
        o = self.off
        assert o + n <= self.cols, f"arena overflow: {o}+{n} > {self.cols}"
        self.off = o + n
        self.hw = max(self.hw, self.off)
        return o

    def push(self):
        self.saved.append(self.off)

    def pop(self):
        self.off = self.saved.pop()


def build_nc(cfg, dbg=False):
    D, NB, H, F = cfg["D"], cfg["NB"], cfg["H"], cfg["F"]
    NO = NB // 2
    DC = D // P
    FC = F // P
    MT = NB // P
    TT = NO // P
    HPn = H // 2
    assert HPn == DC, "head-pair count must equal d-chunk count for the spill layout"
    NCH = NO // 512
    MC1 = NB // 512
    TS2 = NO // 256
    WT = HPn * 130
    NFO = max(1, D // 512)
    FOW = D // NFO

    nc = bass.Bass("TRN2", target_bir_lowering=False, detect_race_conditions=False,
                   dynamic_dma_scratch_size=8192)

    xin = nc.dram_tensor("xin", [P, MT * D], F32, kind="ExternalInput").ap()
    wqT = nc.dram_tensor("wqT", [P, DC * D], F32, kind="ExternalInput").ap()
    wkT = nc.dram_tensor("wkT", [P, DC * D], F32, kind="ExternalInput").ap()
    wvT = nc.dram_tensor("wvT", [P, DC * D], F32, kind="ExternalInput").ap()
    woT = nc.dram_tensor("woT", [P, DC * D], F32, kind="ExternalInput").ap()
    bqc = nc.dram_tensor("bqc", [P, DC], F32, kind="ExternalInput").ap()
    bkc = nc.dram_tensor("bkc", [P, DC], F32, kind="ExternalInput").ap()
    bvc = nc.dram_tensor("bvc", [P, DC], F32, kind="ExternalInput").ap()
    borow = nc.dram_tensor("borow", [1, D], F32, kind="ExternalInput").ap()
    w1n = nc.dram_tensor("w1n", [P, FC * D], F32, kind="ExternalInput").ap()
    w1t = nc.dram_tensor("w1t", [P, DC * F], F32, kind="ExternalInput").ap()
    b1c = nc.dram_tensor("b1c", [P, FC], F32, kind="ExternalInput").ap()
    w2n = nc.dram_tensor("w2n", [P, DC * F], F32, kind="ExternalInput").ap()
    w2t = nc.dram_tensor("w2t", [P, FC * D], F32, kind="ExternalInput").ap()
    b2row = nc.dram_tensor("b2row", [1, D], F32, kind="ExternalInput").ap()
    out = nc.dram_tensor("out", [P, TT * D], F32, kind="ExternalOutput").ap()

    s_l1h = nc.dram_tensor("s_l1h", [P, DC * NB], BF16).ap()
    s_l1l = nc.dram_tensor("s_l1l", [P, DC * NB], BF16).ap()
    s_qh = nc.dram_tensor("s_qh", [P, DC * NB], BF16).ap()
    s_ql = nc.dram_tensor("s_ql", [P, DC * NB], BF16).ap()
    s_kh = nc.dram_tensor("s_kh", [P, DC * NO], BF16).ap()
    s_kl = nc.dram_tensor("s_kl", [P, DC * NO], BF16).ap()
    s_vt = nc.dram_tensor("s_vt", [P, HPn * MT * 130], F32).ap()
    s_oh = nc.dram_tensor("s_oh", [64, HPn * 2 * NO], BF16).ap()
    s_ol = nc.dram_tensor("s_ol", [64, HPn * 2 * NO], BF16).ap()
    s_x2 = nc.dram_tensor("s_x2", [P, TT * D], F32).ap()
    s_h = nc.dram_tensor("s_h", [P, FC * NO], F32).ap()
    s_w1c = nc.dram_tensor("s_w1c", [P, DC * F], BF16).ap()
    s_w2c = nc.dram_tensor("s_w2c", [P, FC * D], BF16).ap()
    cc1i = nc.dram_tensor("cc1i", [1, 16], F32).ap()
    cc1o = nc.dram_tensor("cc1o", [1, 16], F32).ap()
    cc2i = nc.dram_tensor("cc2i", [1, 16], F32).ap()
    cc2o = nc.dram_tensor("cc2o", [1, 16], F32).ap()

    kb = KB(nc)

    with contextlib.ExitStack() as stack:
        ent = stack.enter_context
        base_af = 2 * P + 16 + 3 * D + 64
        AFN = base_af + max(DC * D + 9 * D,
                            2 * MT * 130 + 2 * 1024 + 2 * NO + 2112 + 64
                            + 4 * D,
                            DC * D + 6 * D,
                            5 * D + NO + 2 * FC * 256)
        ABN = P + max(4 * DC * D + 4 * DC * 512 + 4 * 512,
                      2 * DC * D + 2 * HPn * NO,
                      4 * NB + 4 * NO + NB + 4 * NO + 1 + 8 + 2 * D,
                      DC * NO + D + DC * F)
        af_t = ent(nc.sbuf_tensor("AF", [P, AFN], F32))
        ab_t = ent(nc.sbuf_tensor("AB", [P, ABN], BF16))
        sc_t = ent(nc.sbuf_tensor("SC", [P, 768], F32))
        AF = Arena(AFN)
        AB = Arena(ABN)
        SC = Arena(768)
        sc = sc_t

        ps_a = ent(nc.psum_tensor("ps_a", [P, 1024], F32))
        ps_b = ent(nc.psum_tensor("ps_b", [P, 1024], F32))
        ps_pv = ent(nc.psum_tensor("ps_pv", [P, 1024], F32))
        ps_x = ent(nc.psum_tensor("ps_x", [P, 512], F32))
        ps_xb = ent(nc.psum_tensor("ps_xb", [P, 1024], BF16))

        for s in ("ld", "sp", "gd", "T", "V", "S", "G", "cc"):
            kb.sem_handles[s] = ent(nc.semaphore(s))

        def A(off, n):
            return af_t[:, off:off + n]

        def Ab(off, n):
            return ab_t[:, off:off + n]

        o_stats = SC.alloc(64)
        o_mu2 = SC.alloc(TT)
        o_r2 = SC.alloc(TT)
        o_sw1 = SC.alloc(FC)
        o_sw2 = SC.alloc(DC)
        o_b1q = SC.alloc(FC)
        o_sc1 = SC.alloc(FC)
        o_misc = SC.alloc(32)
        o_negc = SC.alloc(2)
        o_lmax = SC.alloc(8)
        o_bq = SC.alloc(DC)
        o_bk = SC.alloc(DC)
        o_bv = SC.alloc(DC)

        def dma_to(dst, src):
            def fn(e):
                return e.dma_start(out=dst, in_=src)
            return fn

        o_idf = AF.alloc(P)
        o_ccb = AF.alloc(16)
        o_sw2row = AF.alloc(D)
        o_rb2 = AF.alloc(D)
        o_b2r = AF.alloc(D)
        o_idb = AB.alloc(P)

        ev_b = kb.step("sync", dma_to(sc[:, o_bq:o_bq + DC], bqc), sig="ld", sig_n=16)
        kb.step("sync", dma_to(sc[:, o_bk:o_bk + DC], bkc), sig="ld", sig_n=16)
        ev_bv_ = kb.step("sync", dma_to(sc[:, o_bv:o_bv + DC], bvc), sig="ld", sig_n=16)
        ev_b1c = kb.step("sync", dma_to(sc[:, o_b1q:o_b1q + FC], b1c), sig="ld", sig_n=16)

        def mk_ident(e):
            e.memset(A(o_idf, P), 0.0)
            e.affine_select(out=A(o_idf, P), in_=A(o_idf, P),
                            compare_op=Alu.not_equal, fill=1.0,
                            base=0, pattern=[[-1, P]], channel_multiplier=1)
            e.memset(Ab(o_idb, P), 0.0)
            return e.affine_select(out=Ab(o_idb, P), in_=Ab(o_idb, P),
                                   compare_op=Alu.not_equal, fill=1.0,
                                   base=0, pattern=[[-1, P]], channel_multiplier=1)
        o_ones0 = AF.alloc(P)
        def mk_ones0(e):
            return e.memset(af_t[0:1, o_ones0:o_ones0 + P], 1.0)
        ev_o0 = kb.step("vector", mk_ones0, sig="V")

        ev_id = kb.step("gpsimd", mk_ident, sig="G")

        def bcast_row(src_ap, dst_ap, ncols, wait=()):
            ev = None
            done = 0
            while done < ncols:
                w = min(512, ncols - done)
                def mmb(e, done=done, w=w):
                    return e.matmul(ps_x[:, 0:w], af_t[0:1, o_ones0:o_ones0 + P],
                                    src_ap[0:1, done:done + w], start=True, stop=True)
                ev_m = kb.step("tensor", mmb,
                               wait=list(wait) + [ev_o0, kb.mark("V")], sig="T")
                def cpb(e, done=done, w=w):
                    return e.tensor_copy(dst_ap[:, done:done + w], ps_x[:, 0:w])
                ev = kb.step("vector", cpb, wait=[ev_m], sig="V")
                done += w
            return ev

        AF.push(); AB.push()
        o_wstage = AF.alloc(DC * D)
        o_wah = AB.alloc(DC * D); o_wal = AB.alloc(DC * D)
        o_wbh = AB.alloc(DC * D); o_wbl = AB.alloc(DC * D)

        def split_weight(src_dram, dst_hi, dst_lo, prev_ev, stage):
            ev_l = kb.step("sync", dma_to(A(stage, DC * D), src_dram),
                           wait=[prev_ev, kb.mark("V")], sig="ld", sig_n=16)
            def hi(e):
                return e.tensor_copy(Ab(dst_hi, DC * D), A(stage, DC * D))
            kb.step("vector", hi, wait=[ev_l, kb.mark("T")], sig="V")
            def lo(e):
                return e.scalar_tensor_tensor(Ab(dst_lo, DC * D), A(stage, DC * D), 0.0,
                                              Ab(dst_hi, DC * D), Alu.add, Alu.subtract)
            return kb.step("vector", lo, sig="V")

        ev_wq = split_weight(wqT, o_wah, o_wal, None, o_wstage)
        ev_wk = split_weight(wkT, o_wbh, o_wbl, ev_wq, o_wstage)

        o_lnt = AF.alloc(4 * D)
        o_xc0 = AF.alloc(4 * D)
        o_xc1 = AF.alloc(4 * D)
        o_sq = AF.alloc(D)
        o_xc = [o_xc0, o_xc1]
        o_vf = o_lnt
        o_vt = o_xc0
        assert 4 * WT <= 8 * D, "V token chunk alias overflow"
        o_l1h = [AB.alloc(DC * 512) for _ in range(2)]
        o_l1l = [AB.alloc(DC * 512) for _ in range(2)]
        o_eh = AB.alloc(1024); o_el = AB.alloc(1024)

        ev_trpv = [None, None, None, None]
        ev_redpv = [None]
        ev_elopv = [None]

        def ln1_chunk(mc, buf, wait_extra=()):
            ev_x = kb.step("sync", dma_to(A(o_xc[buf], 4 * D),
                                          xin[:, mc * 4 * D:(mc + 1) * 4 * D]),
                           wait=list(wait_extra), sig="ld", sig_n=16)
            ev_e = None
            for tt in range(4):
                xs = A(o_xc[buf] + tt * D, D)
                def sq(e, xs=xs):
                    return e.activation(A(o_sq, D), xs, Act.Square)
                ev_sq = kb.step("scalar", sq, wait=[ev_x, ev_redpv[0]], sig="S")
                s1 = sc[:, o_stats:o_stats + 1]
                s2 = sc[:, o_stats + 1:o_stats + 2]
                mu = sc[:, o_stats + 2:o_stats + 3]
                rr = sc[:, o_stats + 3:o_stats + 4]
                rr2 = sc[:, o_stats + 4:o_stats + 5]
                ev_red = kb.chain("vector", [
                    lambda e, xs=xs, s1=s1: (e.reduce_sum(s1, xs, AX),
                                             e.reduce_sum(s2, A(o_sq, D), AX))[1],
                    lambda e, s1=s1, mu=mu: (e.tensor_scalar(mu, s1, 1.0 / D, None, Alu.mult),
                                             e.tensor_scalar(s2, s2, 1.0 / D, None, Alu.mult))[1],
                    lambda e, mu=mu, s1=s1: e.tensor_tensor(s1, mu, mu, Alu.mult),
                    lambda e, s1=s1, s2=s2: e.tensor_tensor(s2, s2, s1, Alu.subtract),
                    lambda e, s2=s2: e.tensor_scalar(s2, s2, 1e-5, None, Alu.add),
                ], wait=[ev_x, ev_sq])
                ev_redpv[0] = ev_red
                def sq2(e, s2=s2, rr=rr):
                    return e.activation(rr, s2, Act.Sqrt)
                ev_rt = kb.step("scalar", sq2, wait=[ev_red], sig="S")
                ev_ln = kb.chain("vector", [
                    lambda e, rr=rr, rr2=rr2: e.reciprocal(rr2, rr),
                    lambda e, xs=xs, mu=mu, rr2=rr2, tt=tt: e.tensor_scalar(
                        A(o_lnt + tt * D, D), xs, mu, rr2, Alu.subtract, Alu.mult),
                ], wait=[ev_rt, ev_trpv[tt]])
                for dc in range(DC):
                    def tr(e, tt=tt, dc=dc):
                        return e.transpose(ps_x[:, 0:P], A(o_lnt + tt * D + dc * P, P),
                                           A(o_idf, P))
                    ev_t = kb.step("tensor", tr,
                                   wait=[ev_ln, ev_id, ev_elopv[0]], sig="T")
                    ev_trpv[tt] = ev_t
                    def ehi(e, tt=tt, dc=dc, buf=buf):
                        return e.tensor_copy(Ab(o_l1h[buf] + dc * 512 + tt * P, P),
                                             ps_x[:, 0:P])
                    kb.step("vector", ehi, wait=[ev_t], sig="V")
                    def elo(e, tt=tt, dc=dc, buf=buf):
                        return e.scalar_tensor_tensor(Ab(o_l1l[buf] + dc * 512 + tt * P, P),
                                                      ps_x[:, 0:P], 0.0,
                                                      Ab(o_l1h[buf] + dc * 512 + tt * P, P),
                                                      Alu.add, Alu.subtract)
                    ev_e = kb.step("vector", elo, sig="V")
                    ev_elopv[0] = ev_e
            def sp_h(e, mc=mc, buf=buf):
                return e.dma_start(
                    out=s_l1h.rearrange("p (dc nb) -> p dc nb", dc=DC)[:, :, mc * 512:(mc + 1) * 512],
                    in_=Ab(o_l1h[buf], DC * 512).rearrange("p (dc m) -> p dc m", dc=DC))
            kb.step("scalar", sp_h, wait=[ev_e], sig="sp", sig_n=16)
            def sp_l(e, mc=mc, buf=buf):
                return e.dma_start(
                    out=s_l1l.rearrange("p (dc nb) -> p dc nb", dc=DC)[:, :, mc * 512:(mc + 1) * 512],
                    in_=Ab(o_l1l[buf], DC * 512).rearrange("p (dc m) -> p dc m", dc=DC))
            kb.step("scalar", sp_l, sig="sp", sig_n=16)
            return ev_e

        ev_qmm = {}

        def proj_block(buf, whi_o, wlo_o, bias_o, psum, dst_h, dst_l, col0, n_cols_dst,
                       first_wait, qkey=None):
            levac = [None, None]
            lspill = [None, None]
            for fc in range(DC):
                half = fc % 2
                pb = psum[:, half * 512:half * 512 + 512]
                mlast = None
                for dc in range(DC):
                    lnh = Ab(o_l1h[buf] + dc * 512, 512)
                    lnl = Ab(o_l1l[buf] + dc * 512, 512)
                    wh = Ab(whi_o + dc * D + fc * P, P)
                    wl = Ab(wlo_o + dc * D + fc * P, P)
                    w = []
                    if dc == 0:
                        w = list(first_wait) if fc == 0 else []
                        w.append(levac[half])
                    def mm(e, wh=wh, wl=wl, lnh=lnh, lnl=lnl, dc=dc, pb=pb):
                        e.matmul(pb, wh, lnh, start=(dc == 0), stop=False)
                        e.matmul(pb, wh, lnl, start=False, stop=False)
                        return e.matmul(pb, wl, lnh, start=False, stop=(dc == DC - 1))
                    mlast = kb.step("tensor", mm, wait=w, sig="T")
                if qkey is not None:
                    ev_qmm[qkey] = mlast
                bias = sc[:, bias_o + fc:bias_o + fc + 1]
                eh_o = o_eh + half * 512
                el_o = o_el + half * 512
                def ehi(e, bias=bias, pb=pb, eh_o=eh_o):
                    return e.tensor_scalar(Ab(eh_o, 512), pb, bias, None, Alu.add)
                kb.step("vector", ehi, wait=[mlast, ev_b1c, lspill[half]], sig="V")
                def elo(e, bias=bias, pb=pb, eh_o=eh_o, el_o=el_o):
                    return e.scalar_tensor_tensor(Ab(el_o, 512), pb, bias,
                                                  Ab(eh_o, 512), Alu.add, Alu.subtract)
                ev_e = kb.step("vector", elo, sig="V")
                levac[half] = ev_e
                th = dst_h.rearrange("p (dc nb) -> p dc nb", dc=DC)
                tl = dst_l.rearrange("p (dc nb) -> p dc nb", dc=DC)
                def sph(e, fc=fc, th=th, col0=col0, eh_o=eh_o):
                    return e.dma_start(out=th[:, fc, col0:col0 + 512], in_=Ab(eh_o, 512))
                kb.step("scalar", sph, wait=[ev_e], sig="sp", sig_n=16)
                def spl(e, fc=fc, tl=tl, col0=col0, el_o=el_o):
                    return e.dma_start(out=tl[:, fc, col0:col0 + 512], in_=Ab(el_o, 512))
                lspill[half] = kb.step("scalar", spl, sig="sp", sig_n=16)

        ev_lnc = {}
        for mc in range(MC1):
            buf = mc % 2
            wx = [ev_lnc[mc - 2]] if mc >= 2 else []
            ev_ln = ln1_chunk(mc, buf, wait_extra=wx)
            ev_lnc[mc] = ev_ln
            proj_block(buf, o_wah, o_wal, o_bq, ps_a, s_qh, s_ql, mc * 512, NB,
                       [ev_ln, ev_wq], qkey=mc)
            if mc < MC1 // 2:
                proj_block(buf, o_wbh, o_wbl, o_bk, ps_b, s_kh, s_kl, mc * 512, NO,
                           [ev_ln, ev_wk])

        evA1_T = kb.mark("T"); evA1_sp = kb.mark("sp")

        ev_wv = split_weight(wvT, o_wah, o_wal, None, o_wstage)

        ev_vmm = {}
        for mc in range(MC1):
            buf = mc % 2
            def rl_h(e, mc=mc, buf=buf):
                return e.dma_start(
                    out=Ab(o_l1h[buf], DC * 512).rearrange("p (dc m) -> p dc m", dc=DC),
                    in_=s_l1h.rearrange("p (dc nb) -> p dc nb", dc=DC)[:, :, mc * 512:(mc + 1) * 512])
            wrl = [evA1_sp, ev_qmm[MC1 - 2 + mc]] if mc < 2 else [ev_vmm[mc - 2]]
            kb.step("sync", rl_h, wait=wrl, sig="ld", sig_n=16)
            def rl_l(e, mc=mc, buf=buf):
                return e.dma_start(
                    out=Ab(o_l1l[buf], DC * 512).rearrange("p (dc m) -> p dc m", dc=DC),
                    in_=s_l1l.rearrange("p (dc nb) -> p dc nb", dc=DC)[:, :, mc * 512:(mc + 1) * 512])
            ev_rl = kb.step("sync", rl_l, sig="ld", sig_n=16)
            levacV = [None, None]
            for fc in range(DC):
                half = fc % 2
                pb = ps_a[:, half * 512:half * 512 + 512]
                mlast = None
                for dc in range(DC):
                    lnh = Ab(o_l1h[buf] + dc * 512, 512)
                    lnl = Ab(o_l1l[buf] + dc * 512, 512)
                    wh = Ab(o_wah + dc * D + fc * P, P)
                    wl = Ab(o_wal + dc * D + fc * P, P)
                    w = []
                    if dc == 0:
                        w = [ev_rl, ev_wv] if fc == 0 else []
                        w.append(levacV[half])
                        if fc == 0:
                            w.append(kb.mark("V"))
                    def mm(e, wh=wh, wl=wl, lnh=lnh, lnl=lnl, dc=dc, pb=pb):
                        e.matmul(pb, wh, lnh, start=(dc == 0), stop=False)
                        e.matmul(pb, wh, lnl, start=False, stop=False)
                        return e.matmul(pb, wl, lnh, start=False, stop=(dc == DC - 1))
                    mlast = kb.step("tensor", mm, wait=w, sig="T")
                ev_vmm[mc] = mlast
                bias = sc[:, o_bv + fc:o_bv + fc + 1]
                def evf(e, bias=bias, fc=fc, pb=pb):
                    return e.tensor_scalar(A(o_vf + fc * 512, 512), pb,
                                           bias, None, Alu.add)
                levacV[half] = kb.step("vector", evf, wait=[mlast, ev_bv_], sig="V")
            ev_vf = kb.mark("V")
            for tt in range(4):
                mt = mc * 4 + tt
                ev_e2 = None
                for fc in range(DC):
                    def tr(e, fc=fc, tt=tt):
                        return e.transpose(ps_x[:, 0:P], A(o_vf + fc * 512 + tt * P, P),
                                           A(o_idf, P))
                    ev_t = kb.step("tensor", tr, wait=[ev_vf, kb.mark("V")], sig="T")
                    def evc(e, fc=fc, tt=tt):
                        dst = A(o_vt + tt * WT, WT).rearrange("p (hh c) -> p hh c", c=65)
                        return e.tensor_copy(dst[:, 2 * fc:2 * fc + 2, 0:64],
                                             ps_x[:, 0:P].rearrange("p (hh c) -> p hh c", c=64))
                    ev_e2 = kb.step("vector", evc, wait=[ev_t], sig="V")
                def ones_(e, tt=tt):
                    dst = A(o_vt + tt * WT, WT).rearrange("p (hh c) -> p hh c", c=65)
                    return e.memset(dst[:, :, 64:65], 1.0)
                ev_o = kb.step("vector", ones_, wait=[ev_e2], sig="V")
                def spv(e, mt=mt, tt=tt):
                    dst = s_vt.rearrange("p (hp mt c) -> p hp mt c", hp=HPn, mt=MT)
                    src = A(o_vt + tt * WT, WT).rearrange("p (hp c) -> p hp c", c=130)
                    return e.dma_start(out=dst[:, :, mt, :], in_=src)
                kb.step("scalar", spv, wait=[ev_o], sig="sp", sig_n=16)

        evA_T = kb.mark("T"); evA_sp = kb.mark("sp"); evA_V = kb.mark("V")
        AF.pop(); AB.pop()

        AF.push(); AB.push()
        o_vtk2 = [AF.alloc(MT * 130) for _ in range(2)]
        o_pb = [AF.alloc(1024) for _ in range(2)]
        o_ohp = AF.alloc(2 * NO)
        o_rb = AF.alloc(1024)
        o_rbs = AF.alloc(1024)
        o_ones64 = AF.alloc(64)
        o_wn2 = [AF.alloc(D) for _ in range(2)]
        o_wrbc = AF.alloc(D)
        o_wrow = AF.alloc(D)
        o_qhi2 = [AB.alloc(NB) for _ in range(2)]
        o_qlo2 = [AB.alloc(NB) for _ in range(2)]
        o_khi2 = [AB.alloc(NO) for _ in range(2)]
        o_klo2 = [AB.alloc(NO) for _ in range(2)]
        o_q2 = AB.alloc(NB)
        o_oh = AB.alloc(2 * NO); o_ol = AB.alloc(2 * NO)
        o_onesb = AB.alloc(1)
        o_wcod2 = [AB.alloc(D) for _ in range(2)]
        def mkones(e):
            e.memset(Ab(o_onesb, 1), 1.0)
            return e.memset(af_t[64:65, o_ones64:o_ones64 + 64], 1.0)
        ev_ones = kb.step("vector", mkones, wait=[evA_V, evA_T], sig="V")

        m2 = sc[0:1, o_misc:o_misc + 1]
        mprod = [sc[0:1, o_misc + 1 + i:o_misc + 2 + i] for i in range(4)]
        t0 = sc[0:1, o_misc + 5:o_misc + 6]
        ld_ev = {}
        cb_ev = {}

        def emit_loads(hp, wait0):
            b = hp % 2
            ev_lq = kb.step("sync", dma_to(Ab(o_qhi2[b], NB), s_qh[:, hp * NB:(hp + 1) * NB]),
                            wait=wait0, sig="ld", sig_n=16)
            kb.step("sync", dma_to(Ab(o_qlo2[b], NB), s_ql[:, hp * NB:(hp + 1) * NB]),
                    sig="ld", sig_n=16)
            kb.step("sync", dma_to(Ab(o_khi2[b], NO), s_kh[:, hp * NO:(hp + 1) * NO]),
                    sig="ld", sig_n=16)
            ev_lk = kb.step("sync", dma_to(Ab(o_klo2[b], NO), s_kl[:, hp * NO:(hp + 1) * NO]),
                            sig="ld", sig_n=16)
            ev_lv = kb.step("sync", dma_to(A(o_vtk2[b], MT * 130),
                                           s_vt[:, hp * MT * 130:(hp + 1) * MT * 130]),
                            sig="ld", sig_n=16)
            ld_ev[hp] = (ev_lq, ev_lk, ev_lv)

        def emit_norm_max(slot, ncols, head, ev_src):
            first = True
            for chunk in range(ncols // 512):
                def omm(e, chunk=chunk, head=head):
                    return e.matmul(ps_x[0:1, 0:512],
                                    Ab(o_onesb, 1)[head * 64:head * 64 + 64, 0:1],
                                    Ab(o_q2 + chunk * 512, 512)[head * 64:head * 64 + 64, :],
                                    start=True, stop=True,
                                    tile_position=(head * 64, 0))
                ev_m = kb.step("tensor", omm, wait=[ev_src, ev_ones, kb.mark("V")],
                               sig="T")
                if first:
                    kb.chain("vector", [
                        lambda e, slot=slot: e.reduce_max(slot, ps_x[0:1, 0:512], AX),
                    ], wait=[ev_m])
                else:
                    kb.chain("vector", [
                        lambda e: e.reduce_max(t0, ps_x[0:1, 0:512], AX),
                        lambda e, slot=slot: e.tensor_tensor(slot, slot, t0, Alu.max),
                    ], wait=[ev_m])
                first = False

        def emit_cbias(hp):
            b = hp % 2
            ev_lq, ev_lk, ev_lv = ld_ev[hp]
            def sqq(e, b=b):
                return e.tensor_tensor(Ab(o_q2, NB), Ab(o_qhi2[b], NB),
                                       Ab(o_qhi2[b], NB), Alu.mult)
            ev_s = kb.step("gpsimd", sqq, wait=[ev_lq, kb.mark("V")], sig="G")
            emit_norm_max(mprod[0], NB, 0, ev_s)
            emit_norm_max(mprod[1], NB, 1, ev_s)
            def sqk(e, b=b):
                return e.tensor_tensor(Ab(o_q2, NO), Ab(o_khi2[b], NO),
                                       Ab(o_khi2[b], NO), Alu.mult)
            ev_sk = kb.step("gpsimd", sqk, wait=[ev_lk, kb.mark("V")], sig="G")
            emit_norm_max(mprod[2], NO, 0, ev_sk)
            emit_norm_max(mprod[3], NO, 1, ev_sk)
            ev_c1 = kb.chain("vector", [
                lambda e: e.tensor_tensor(mprod[0], mprod[0], mprod[2], Alu.mult),
                lambda e: e.tensor_tensor(mprod[1], mprod[1], mprod[3], Alu.mult),
                lambda e: e.tensor_tensor(m2, mprod[0], mprod[1], Alu.max),
            ])
            def csqrt(e):
                return e.activation(m2, m2, Act.Sqrt)
            ev_c2 = kb.step("scalar", csqrt, wait=[ev_c1], sig="S")
            def cneg(e):
                return e.tensor_scalar(m2, m2, -1.02 / 32.0, None, Alu.mult)
            ev_c3 = kb.step("vector", cneg, wait=[ev_c2], sig="V")
            cb_ev[hp] = bcast_row(sc[:, o_misc:o_misc + 1],
                                  sc[:, o_negc + b:o_negc + b + 1], 1, wait=[ev_c3])


        JP = D // P
        wn_ev = [[evA_sp, evA_T, evA_V], [evA_sp, evA_T, evA_V]]
        wcod_ev = [[evA_T, evA_sp], [evA_T, evA_sp]]
        wsh = {}
        W_TASKS = []
        wtmp = sc[:, o_misc + 6:o_misc + 7]

        def t_w1max(fcc, bi):
            def task():
                ev_l = kb.step("sync",
                               dma_to(A(o_wn2[bi], D), w1n[:, fcc * D:(fcc + 1) * D]),
                               wait=wn_ev[bi], sig="ld", sig_n=16)
                def rmax(e, fcc=fcc, bi=bi):
                    return e.reduce_max(sc[:, o_sw1 + fcc:o_sw1 + fcc + 1],
                                        A(o_wn2[bi], D), AX, apply_absolute_value=True)
                wn_ev[bi] = [kb.step("vector", rmax, wait=[ev_l], sig="V")]
            return task

        def t_swfin(key, off, n):
            def task():
                def fin(e):
                    return e.tensor_scalar(sc[:, off:off + n], sc[:, off:off + n],
                                           1e-8, 1.0 / 127.0, Alu.max, Alu.mult)
                wsh[key] = kb.step("vector", fin, wait=wn_ev[0] + wn_ev[1], sig="V")
            return task

        def t_w1row(p):
            def task():
                for j in range(JP):
                    fcc = p * JP + j
                    def trs(e, fcc=fcc):
                        return e.transpose(ps_x[0:1, 0:P],
                                           sc[:, o_sw1 + fcc:o_sw1 + fcc + 1],
                                           A(o_idf, P))
                    ev_t = kb.step("tensor", trs, wait=[wsh["sw1"], kb.mark("V")],
                                   sig="T")
                    def cps(e, j=j):
                        return e.tensor_copy(
                            af_t[0:1, o_wrow + j * P:o_wrow + (j + 1) * P],
                            ps_x[0:1, 0:P])
                    kb.step("vector", cps, wait=[ev_t], sig="V")
                def rrow(e):
                    return e.reciprocal(af_t[0:1, o_wrow:o_wrow + D],
                                        af_t[0:1, o_wrow:o_wrow + D])
                ev_rr = kb.step("vector", rrow, wait=[kb.mark("V")], sig="V")
                wsh["bc"] = bcast_row(af_t[:, o_wrow:o_wrow + D], A(o_wrbc, D), D,
                                      wait=[ev_rr])
            return task

        def t_w2max(doc, p, bi):
            def task():
                ev_l = kb.step("sync",
                               dma_to(A(o_wn2[bi], D),
                                      w2n[:, doc * F + p * D:doc * F + (p + 1) * D]),
                               wait=wn_ev[bi], sig="ld", sig_n=16)
                dst = sc[:, o_sw2 + doc:o_sw2 + doc + 1]
                if p == 0:
                    def rmax(e, bi=bi, dst=dst):
                        return e.reduce_max(dst, A(o_wn2[bi], D), AX,
                                            apply_absolute_value=True)
                    wn_ev[bi] = [kb.step("vector", rmax, wait=[ev_l], sig="V")]
                else:
                    ev = kb.chain("vector", [
                        lambda e, bi=bi: e.reduce_max(wtmp, A(o_wn2[bi], D), AX,
                                                      apply_absolute_value=True),
                        lambda e, dst=dst: e.tensor_tensor(dst, dst, wtmp, Alu.max),
                    ], wait=[ev_l])
                    wn_ev[bi] = [ev]
            return task

        def t_w2row():
            def task():
                for doc in range(DC):
                    def trs2(e, doc=doc):
                        return e.transpose(ps_x[0:1, 0:P],
                                           sc[:, o_sw2 + doc:o_sw2 + doc + 1],
                                           A(o_idf, P))
                    ev_t = kb.step("tensor", trs2, wait=[wsh["sw2"], kb.mark("V")],
                                   sig="T")
                    def cps2(e, doc=doc):
                        return e.tensor_copy(
                            af_t[0:1, o_sw2row + doc * P:o_sw2row + (doc + 1) * P],
                            ps_x[0:1, 0:P])
                    kb.step("vector", cps2, wait=[ev_t], sig="V")
                def rrow2(e):
                    return e.reciprocal(af_t[0:1, o_rb2:o_rb2 + D],
                                        af_t[0:1, o_sw2row:o_sw2row + D])
                ev_rr = kb.step("vector", rrow2, wait=[kb.mark("V")], sig="V")
                wsh["bc"] = bcast_row(af_t[:, o_rb2:o_rb2 + D], A(o_wrbc, D), D,
                                      wait=[ev_rr])
            return task

        def t_wq(coff, bi, wt_src, s_dst, scat=None):
            def task():
                ev_l = kb.step("sync",
                               dma_to(A(o_wn2[bi], D), wt_src[:, coff:coff + D]),
                               wait=wn_ev[bi], sig="ld", sig_n=16)
                def q1(e, bi=bi):
                    return e.tensor_tensor(A(o_wn2[bi], D), A(o_wn2[bi], D),
                                           A(o_wrbc, D), Alu.mult)
                kb.step("vector", q1, wait=[ev_l, wsh["bc"]], sig="V")
                def q2(e, bi=bi):
                    return e.tensor_scalar(A(o_wn2[bi], D), A(o_wn2[bi], D),
                                           MAGIC, MAGIC, Alu.add, Alu.subtract)
                kb.step("vector", q2, sig="V")
                def q3(e, bi=bi):
                    return e.tensor_scalar(Ab(o_wcod2[bi], D), A(o_wn2[bi], D),
                                           -128.0, 127.0, Alu.max, Alu.min)
                ev_q = kb.step("vector", q3, wait=wcod_ev[bi], sig="V")
                wn_ev[bi] = [ev_q]
                if scat is None:
                    dst = s_dst[:, coff:coff + D]
                    srcc = Ab(o_wcod2[bi], D)
                else:
                    p_, dc_ = scat
                    dst = s_dst.rearrange("p (fc dd) -> p fc dd",
                                          dd=D)[:, p_ * JP:(p_ + 1) * JP,
                                                dc_ * P:(dc_ + 1) * P]
                    srcc = Ab(o_wcod2[bi], D).rearrange("p (j c) -> p j c", c=P)
                wcod_ev[bi] = [kb.step("scalar",
                                       dma_to(dst, srcc),
                                       wait=[ev_q], sig="sp", sig_n=16)]
            return task

        wni = [0]
        def nb2():
            b = wni[0] % 2
            wni[0] += 1
            return b

        for fcc in range(FC):
            W_TASKS.append(t_w1max(fcc, nb2()))
        W_TASKS.append(t_swfin("sw1", o_sw1, FC))
        for p_ in range(F // D):
            W_TASKS.append(t_w1row(p_))
            for dc in range(DC):
                W_TASKS.append(t_wq(dc * F + p_ * D, nb2(), w1t, s_w1c,
                                    scat=(p_, dc)))
        for doc in range(DC):
            for p_ in range(F // D):
                W_TASKS.append(t_w2max(doc, p_, nb2()))
        W_TASKS.append(t_swfin("sw2", o_sw2, DC))
        W_TASKS.append(t_w2row())
        for fcc in range(FC):
            W_TASKS.append(t_wq(fcc * D, nb2(), w2t, s_w2c))

        NPPN = 1 + sum(1 for mt_ in range(MT) if mt_ % 8 == 7 and mt_ < MT - 1)
        W_SLOTS = HPn * NCH * NPPN
        w_done = [0]
        w_slot = [0]

        def pump_w():
            w_slot[0] += 1
            target = (len(W_TASKS) * w_slot[0]) // W_SLOTS
            while w_done[0] < min(target, len(W_TASKS)):
                W_TASKS[w_done[0]]()
                w_done[0] += 1

        emit_loads(0, [evA_sp, evA_T, kb.mark("V"), kb.mark("S")])
        emit_cbias(0)

        ev_exp_prev = [None, None]
        ev_pv_prev = [None, None]
        for hp in range(HPn):
            b = hp % 2
            ev_lq, ev_lk, ev_lv = ld_ev[hp]
            ev_cb = cb_ev[hp]
            o_qhi, o_qlo = o_qhi2[b], o_qlo2[b]
            o_khi, o_klo = o_khi2[b], o_klo2[b]
            o_vtk = o_vtk2[b]
            negc_ap = sc[:, o_negc + b:o_negc + b + 1]

            for nch in range(NCH):
                ev_evac_prev = kb.mark("V")
                pend = None

                def emit_pv(mt, pbuf, ev_exp):
                    def pvm(e, mt=mt, pbuf=pbuf, o_vtk=o_vtk):
                        r = None
                        for head in range(2):
                            lhs = A(o_vtk + mt * 130 + head * 65, 65)
                            rhs = A(o_pb[pbuf] + head * 512, 512)
                            o = ps_pv[0:65, head * 512:head * 512 + 512]
                            r = e.matmul(o, lhs, rhs, start=(mt == 0), stop=(mt == MT - 1))
                        return r
                    w = [ev_exp, ev_lv]
                    if mt == 0:
                        w.append(ev_evac_prev)
                    ev = kb.step("tensor", pvm, wait=w, sig="T")
                    ev_pv_prev[pbuf] = ev
                    return ev

                for mt in range(MT):
                    pbuf = mt % 2
                    pspair = ps_a if pbuf == 0 else ps_b
                    def smm(e, nch=nch, mt=mt, pspair=pspair, o_qhi=o_qhi, o_qlo=o_qlo,
                            o_khi=o_khi, o_klo=o_klo):
                        r = None
                        for head in range(2):
                            qh_ = Ab(o_qhi + mt * P, P)[head * 64:head * 64 + 64, :]
                            ql_ = Ab(o_qlo + mt * P, P)[head * 64:head * 64 + 64, :]
                            kh_ = Ab(o_khi + nch * 512, 512)[head * 64:head * 64 + 64, :]
                            kl_ = Ab(o_klo + nch * 512, 512)[head * 64:head * 64 + 64, :]
                            o = pspair[:, head * 512:(head + 1) * 512]
                            tp = (head * 64, 0)
                            e.matmul(o, qh_, kh_, start=True, stop=False, tile_position=tp)
                            e.matmul(o, qh_, kl_, start=False, stop=False, tile_position=tp)
                            r = e.matmul(o, ql_, kh_, start=False, stop=True, tile_position=tp)
                        return r
                    w = [ev_lq, ev_lk, ev_exp_prev[pbuf]]
                    ev_sT = kb.step("tensor", smm, wait=w, sig="T")
                    def expf(e, pbuf=pbuf, pspair=pspair, negc_ap=negc_ap):
                        return e.activation(A(o_pb[pbuf], 1024), pspair[:, 0:1024], Act.Exp,
                                            bias=negc_ap, scale=1.0 / 32.0)
                    ev_exp = kb.step("scalar", expf,
                                     wait=[ev_sT, ev_pv_prev[pbuf], ev_cb], sig="S")
                    ev_exp_prev[pbuf] = ev_exp
                    if pend is not None:
                        emit_pv(*pend)
                    pend = (mt, pbuf, ev_exp)
                    if mt % 8 == 7 and mt < MT - 1:
                        pump_w()
                ev_pv = emit_pv(*pend)
                ev_bc_prev = None
                for head in range(2):
                    rbrow = o_rb + head * 512
                    rbsb = o_rbs + head * 512
                    def rcp2(e, head=head, rbrow=rbrow):
                        return e.reciprocal(af_t[64:65, rbrow:rbrow + 512],
                                            ps_pv[64:65, head * 512:head * 512 + 512])
                    ev_r = kb.step("vector", rcp2, wait=[ev_pv], sig="V")
                    def bcmm(e, rbrow=rbrow):
                        return e.matmul(ps_x[0:64, 0:512],
                                        af_t[64:65, o_ones64:o_ones64 + 64],
                                        af_t[64:65, rbrow:rbrow + 512],
                                        start=True, stop=True)
                    wbc = [ev_r, ev_ones, kb.mark("V")] if head == 0 else [ev_r, ev_bc_prev]
                    ev_bb = kb.step("tensor", bcmm, wait=wbc, sig="T")
                    def evbc(e, rbsb=rbsb):
                        return e.tensor_copy(A(rbsb, 512)[0:64, :], ps_x[0:64, 0:512])
                    ev_bc_prev = kb.step("vector", evbc, wait=[ev_bb], sig="V")
                    def onorm(e, head=head, nch=nch, rbsb=rbsb):
                        dst = A(o_ohp + head * NO + nch * 512, 512)[0:64, :]
                        return e.tensor_tensor(dst, ps_pv[0:64, head * 512:head * 512 + 512],
                                               A(rbsb, 512)[0:64, :], Alu.mult)
                    ev_on = kb.step("vector", onorm, wait=[ev_bc_prev], sig="V")
                pump_w()
                if nch == min(1, NCH - 1) and hp + 1 < HPn:
                    emit_loads(hp + 1, [kb.mark("T"), kb.mark("S")])
                    emit_cbias(hp + 1)
            def oh_(e):
                return e.tensor_copy(Ab(o_oh, 2 * NO)[0:64, :], A(o_ohp, 2 * NO)[0:64, :])
            kb.step("vector", oh_, wait=[kb.mark("sp")], sig="V")
            def ol_(e):
                return e.scalar_tensor_tensor(Ab(o_ol, 2 * NO)[0:64, :],
                                              A(o_ohp, 2 * NO)[0:64, :], 0.0,
                                              Ab(o_oh, 2 * NO)[0:64, :],
                                              Alu.add, Alu.subtract)
            ev_sp2 = kb.step("vector", ol_, sig="V")
            kb.step("scalar", dma_to(s_oh[:, hp * 2 * NO:(hp + 1) * 2 * NO],
                                     Ab(o_oh, 2 * NO)[0:64, :]),
                    wait=[ev_sp2], sig="sp", sig_n=16)
            kb.step("scalar", dma_to(s_ol[:, hp * 2 * NO:(hp + 1) * 2 * NO],
                                     Ab(o_ol, 2 * NO)[0:64, :]),
                    sig="sp", sig_n=16)

        while w_done[0] < len(W_TASKS):
            W_TASKS[w_done[0]]()
            w_done[0] += 1
        evB_T = kb.mark("T"); evB_sp = kb.mark("sp"); evB_V = kb.mark("V")
        evW_sp = kb.mark("sp")
        AF.pop(); AB.pop()

        AF.push(); AB.push()
        o_wst2 = AF.alloc(DC * D)
        o_woh = AB.alloc(DC * D); o_wol = AB.alloc(DC * D)
        ev_wo = split_weight(woT, o_woh, o_wol, None, o_wst2)
        o_obh = AB.alloc(HPn * NO)
        o_obl = AB.alloc(HPn * NO)
        ev_rolist = {}
        for hp in range(HPn):
            for j in range(2):
                def roh(e, hp=hp, j=j):
                    return e.dma_start(
                        out=ab_t[64 * j:64 * (j + 1), o_obh + hp * NO:o_obh + (hp + 1) * NO],
                        in_=s_oh[:, (2 * hp + j) * NO:(2 * hp + j + 1) * NO])
                w = [evB_sp, evB_T, evB_V, evW_sp] if (hp == 0 and j == 0) else []
                kb.step("sync", roh, wait=w, sig="ld", sig_n=16)
                def rol(e, hp=hp, j=j):
                    return e.dma_start(
                        out=ab_t[64 * j:64 * (j + 1), o_obl + hp * NO:o_obl + (hp + 1) * NO],
                        in_=s_ol[:, (2 * hp + j) * NO:(2 * hp + j + 1) * NO])
                ev_rolist[hp] = kb.step("sync", rol, sig="ld", sig_n=16)
        o_bo = AF.alloc(D)
        o_borow2 = AF.alloc(D)
        ev_lbo = kb.step("sync", dma_to(af_t[0:1, o_borow2:o_borow2 + D], borow),
                         sig="ld", sig_n=16)
        ev_bo = bcast_row(af_t[:, o_borow2:o_borow2 + D], A(o_bo, D), D,
                          wait=[ev_lbo])
        o_xt = AF.alloc(D)
        o_x2t = AF.alloc(D)
        o_sq2 = AF.alloc(D)
        o_lnsc = AF.alloc(D)
        def initlmax(e):
            return e.memset(sc[:, o_lmax:o_lmax + 1], -1e30)
        ev_lm0 = kb.step("vector", initlmax, sig="V")
        levacC = [None, None]
        gC = 0
        for tt in range(TT):
            ev_lx = kb.step("sync", dma_to(A(o_xt, D), xin[:, tt * D:(tt + 1) * D]),
                            wait=[kb.mark("V")], sig="ld", sig_n=16)
            def xpb(e):
                return e.tensor_tensor(A(o_xt, D), A(o_xt, D), A(o_bo, D), Alu.add)
            ev_xp = kb.step("vector", xpb, wait=[ev_lx, ev_bo], sig="V")
            for fo in range(NFO):
                half = gC % 2
                gC += 1
                pb = ps_a[:, half * 512:half * 512 + FOW]
                mlast = None
                for fc in range(DC):
                    ohh = Ab(o_obh + fc * NO + tt * P, P)
                    oll = Ab(o_obl + fc * NO + tt * P, P)
                    wh = Ab(o_woh + fc * D + fo * FOW, FOW)
                    wl = Ab(o_wol + fc * D + fo * FOW, FOW)
                    w = []
                    if fc == 0:
                        w = [ev_wo, kb.mark("V")] if gC == 1 else [levacC[half]]
                    w.append(ev_rolist[fc % HPn])
                    def mm(e, ohh=ohh, oll=oll, wh=wh, wl=wl, fc=fc, pb=pb):
                        e.matmul(pb, ohh, wh, start=(fc == 0), stop=False)
                        e.matmul(pb, ohh, wl, start=False, stop=False)
                        return e.matmul(pb, oll, wh, start=False, stop=(fc == DC - 1))
                    mlast = kb.step("tensor", mm, wait=w, sig="T")
                def evx2(e, fo=fo, pb=pb):
                    return e.tensor_tensor(A(o_x2t + fo * FOW, FOW), pb,
                                           A(o_xt + fo * FOW, FOW), Alu.add)
                levacC[half] = kb.step("vector", evx2,
                                       wait=[mlast, ev_xp, kb.mark("sp")], sig="V")
            def sq2f(e):
                return e.activation(A(o_sq2, D), A(o_x2t, D), Act.Square)
            ev_q2 = kb.step("scalar", sq2f, wait=[kb.mark("V")], sig="S")
            s1 = sc[:, o_stats:o_stats + 1]; s2 = sc[:, o_stats + 1:o_stats + 2]
            mu2c = sc[:, o_mu2 + tt:o_mu2 + tt + 1]
            r2c = sc[:, o_r2 + tt:o_r2 + tt + 1]
            ev_st2 = kb.chain("vector", [
                lambda e: (e.reduce_sum(s1, A(o_x2t, D), AX),
                           e.reduce_sum(s2, A(o_sq2, D), AX))[1],
                lambda e, mu2c=mu2c: (e.tensor_scalar(mu2c, s1, 1.0 / D, None, Alu.mult),
                                      e.tensor_scalar(s2, s2, 1.0 / D, None, Alu.mult))[1],
                lambda e, mu2c=mu2c: e.tensor_tensor(s1, mu2c, mu2c, Alu.mult),
                lambda e: e.tensor_tensor(s2, s2, s1, Alu.subtract),
                lambda e: e.tensor_scalar(s2, s2, 1e-5, None, Alu.add),
            ], wait=[ev_q2])
            def rt2(e):
                return e.activation(s2, s2, Act.Sqrt)
            ev_r2 = kb.step("scalar", rt2, wait=[ev_st2], sig="S")
            ev_f2 = kb.chain("vector", [
                lambda e, r2c=r2c: e.reciprocal(r2c, s2),
                lambda e, mu2c=mu2c, r2c=r2c: e.tensor_scalar(
                    A(o_lnsc, D), A(o_x2t, D), mu2c, r2c, Alu.subtract, Alu.mult),
                lambda e: e.reduce_max(s1, A(o_lnsc, D), AX),
                lambda e: e.tensor_tensor(sc[:, o_lmax:o_lmax + 1],
                                          sc[:, o_lmax:o_lmax + 1], s1, Alu.max),
            ], wait=[ev_r2, ev_lm0])
            kb.step("scalar", dma_to(s_x2[:, tt * D:(tt + 1) * D], A(o_x2t, D)),
                    wait=[ev_f2], sig="sp", sig_n=16)

        def trmax(e):
            return e.transpose(ps_x[0:1, 0:P], sc[:, o_lmax:o_lmax + 1], A(o_idf, P))
        ev_tm = kb.step("tensor", trmax, wait=[kb.mark("V")], sig="T")
        def redmax(e):
            return e.reduce_max(sc[0:1, o_lmax + 1:o_lmax + 2], ps_x[0:1, 0:P], AX)
        ev_rm = kb.step("vector", redmax, wait=[ev_tm], sig="V")
        ev_cc0 = kb.chain("vector", [
            lambda e: e.memset(af_t[0:1, o_ccb:o_ccb + 16], 0.0),
            lambda e: e.tensor_copy(af_t[0:1, o_ccb:o_ccb + 1],
                                    sc[0:1, o_lmax + 1:o_lmax + 2]),
        ], wait=[ev_rm])
        ev_cw = kb.step("gpsimd", dma_to(cc1i, af_t[0:1, o_ccb:o_ccb + 16]),
                        wait=[ev_cc0], sig="gd", sig_n=16)
        def ccrun(e):
            return e.collective_compute("AllReduce", Alu.max,
                                        replica_groups=[list(range(8))],
                                        ins=[cc1i], outs=[cc1o])
        ev_cc = kb.step("gpsimd", ccrun, wait=[ev_cw], sig="cc")
        ev_cr = kb.step("gpsimd", dma_to(af_t[0:1, o_ccb:o_ccb + 16], cc1o),
                        wait=[ev_cc], sig="gd", sig_n=16)
        ev_s1s = kb.chain("vector", [
            lambda e: e.tensor_scalar(sc[0:1, o_misc:o_misc + 1],
                                      af_t[0:1, o_ccb:o_ccb + 1],
                                      1e-8, 1.0 / 255.0, Alu.max, Alu.mult),
            lambda e: e.reciprocal(sc[0:1, o_misc + 1:o_misc + 2],
                                   sc[0:1, o_misc:o_misc + 1]),
        ], wait=[ev_cr])
        bcast_row(sc[:, o_misc:o_misc + 1], sc[:, o_misc + 2:o_misc + 3], 1,
                  wait=[ev_s1s])
        ev_bs1 = bcast_row(sc[:, o_misc + 1:o_misc + 2], sc[:, o_misc + 3:o_misc + 4], 1,
                           wait=[ev_s1s])
        ev_sc1 = kb.chain("vector", [
            lambda e: e.tensor_scalar(sc[:, o_sc1:o_sc1 + FC], sc[:, o_sw1:o_sw1 + FC],
                                      sc[:, o_misc + 2:o_misc + 3], None, Alu.mult),
            lambda e: e.reciprocal(sc[:, o_stats:o_stats + FC], sc[:, o_sc1:o_sc1 + FC]),
            lambda e: e.tensor_tensor(sc[:, o_b1q:o_b1q + FC], sc[:, o_b1q:o_b1q + FC],
                                      sc[:, o_stats:o_stats + FC], Alu.mult),
            lambda e: e.tensor_scalar(sc[:, o_b1q:o_b1q + FC], sc[:, o_b1q:o_b1q + FC],
                                      MAGIC, MAGIC, Alu.add, Alu.subtract),
            lambda e: e.tensor_tensor(sc[:, o_b1q:o_b1q + FC], sc[:, o_b1q:o_b1q + FC],
                                      sc[:, o_sc1:o_sc1 + FC], Alu.mult),
        ], wait=[ev_bs1, wsh["sw1"], ev_b1c])
        evC_sp = kb.mark("sp"); evC_T = kb.mark("T")
        AF.pop(); AB.pop()

        AF.push(); AB.push()
        o_x2t2 = AF.alloc(D)
        o_lnq = AF.alloc(D)
        o_xqT = AB.alloc(DC * NO)
        o_ctk = AB.alloc(D)
        o_w1ct_early = AB.alloc(DC * F)
        ev_rw1 = kb.step("scalar", dma_to(Ab(o_w1ct_early, DC * F), s_w1c[:, :]),
                         wait=[evW_sp, evC_T, evC_sp], sig="ld", sig_n=16)
        for tt in range(TT):
            ev_l = kb.step("sync", dma_to(A(o_x2t2, D), s_x2[:, tt * D:(tt + 1) * D]),
                           wait=[evC_sp, evC_T, kb.mark("V")] if tt == 0 else [kb.mark("V")],
                           sig="ld", sig_n=16)
            def qchain(e, tt=tt):
                e.tensor_scalar(A(o_lnq, D), A(o_x2t2, D),
                                sc[:, o_mu2 + tt:o_mu2 + tt + 1],
                                sc[:, o_r2 + tt:o_r2 + tt + 1], Alu.subtract, Alu.mult)
                e.tensor_scalar(A(o_lnq, D), A(o_lnq, D),
                                sc[:, o_misc + 3:o_misc + 4], MAGIC, Alu.mult, Alu.add)
                e.tensor_scalar(A(o_lnq, D), A(o_lnq, D), MAGIC, 0.0,
                                Alu.subtract, Alu.max)
                return e.tensor_scalar(Ab(o_ctk, D), A(o_lnq, D), 255.0, None, Alu.min)
            ev_q = kb.step("vector", qchain, wait=[ev_l, ev_sc1], sig="V")
            for dc in range(DC):
                def trc(e, dc=dc):
                    return e.transpose(ps_xb[:, 0:P], Ab(o_ctk + dc * P, P), Ab(o_idb, P))
                ev_t = kb.step("tensor", trc, wait=[ev_q, kb.mark("V")], sig="T")
                def evc(e, tt=tt, dc=dc):
                    return e.tensor_copy(Ab(o_xqT + dc * NO + tt * P, P), ps_xb[:, 0:P])
                kb.step("vector", evc, wait=[ev_t], sig="V")
        evD_V = kb.mark("V")

        o_w1ct = o_w1ct_early
        o_hrow = AF.alloc(NO)
        def inith(e):
            return e.memset(sc[:, o_lmax + 2:o_lmax + 3], 0.0)
        ev_h0 = kb.step("vector", inith, sig="V")
        ev_sph_prev = kb.mark("sp")
        ev_w2ld = {}
        levacE = [None, None]
        gE = 0
        for fcc in range(FC):
            for tch in range(NO // 512):
                half = gE % 2
                gE += 1
                pb = ps_a[:, half * 512:half * 512 + 512]
                mlast = None
                for dc in range(DC):
                    lhs = Ab(o_w1ct + fcc * D + dc * P, P)
                    rhs = Ab(o_xqT + dc * NO + tch * 512, 512)
                    w = []
                    if dc == 0:
                        w = [ev_rw1, evD_V] if gE == 1 else [levacE[half]]
                    def mm(e, lhs=lhs, rhs=rhs, dc=dc, pb=pb):
                        return e.matmul(pb, lhs, rhs, start=(dc == 0),
                                        stop=(dc == DC - 1))
                    mlast = kb.step("tensor", mm, wait=w, sig="T")
                def evh(e, fcc=fcc, tch=tch, pb=pb):
                    return e.activation(A(o_hrow + tch * 512, 512), pb,
                                        Act.Relu,
                                        bias=sc[:, o_b1q + fcc:o_b1q + fcc + 1],
                                        scale=sc[:, o_sc1 + fcc:o_sc1 + fcc + 1])
                levacE[half] = kb.step("scalar", evh,
                        wait=[mlast, ev_sc1, ev_sph_prev, kb.mark("V")], sig="S")
            ev_hv = kb.mark("S")
            kb.chain("vector", [
                lambda e: e.reduce_max(sc[:, o_stats:o_stats + 1], A(o_hrow, NO), AX),
                lambda e: e.tensor_tensor(sc[:, o_lmax + 2:o_lmax + 3],
                                          sc[:, o_lmax + 2:o_lmax + 3],
                                          sc[:, o_stats:o_stats + 1], Alu.max),
            ], wait=[ev_hv, ev_h0])
            ev_sph_prev = kb.step("scalar", dma_to(s_h[:, fcc * NO:(fcc + 1) * NO],
                                                   A(o_hrow, NO)),
                                  sig="sp", sig_n=16)
            ev_w2ld[fcc] = kb.step("sync",
                                   dma_to(Ab(o_w1ct + fcc * D, D),
                                          s_w2c[:, fcc * D:(fcc + 1) * D]),
                                   wait=[mlast, evW_sp], sig="ld", sig_n=16)

        def trmax2(e):
            return e.transpose(ps_x[0:1, 0:P], sc[:, o_lmax + 2:o_lmax + 3], A(o_idf, P))
        ev_tm2 = kb.step("tensor", trmax2, wait=[kb.mark("V")], sig="T")
        def redmax2(e):
            return e.reduce_max(sc[0:1, o_lmax + 3:o_lmax + 4], ps_x[0:1, 0:P], AX)
        ev_rm2 = kb.step("vector", redmax2, wait=[ev_tm2], sig="V")
        ev_cc20 = kb.chain("vector", [
            lambda e: e.memset(af_t[0:1, o_ccb:o_ccb + 16], 0.0),
            lambda e: e.tensor_copy(af_t[0:1, o_ccb:o_ccb + 1],
                                    sc[0:1, o_lmax + 3:o_lmax + 4]),
        ], wait=[ev_rm2])
        ev_cw2 = kb.step("gpsimd", dma_to(cc2i, af_t[0:1, o_ccb:o_ccb + 16]),
                         wait=[ev_cc20], sig="gd", sig_n=16)
        def ccrun2(e):
            return e.collective_compute("AllReduce", Alu.max,
                                        replica_groups=[list(range(8))],
                                        ins=[cc2i], outs=[cc2o])
        ev_cc2 = kb.step("gpsimd", ccrun2, wait=[ev_cw2], sig="cc")
        ev_cr2 = kb.step("gpsimd", dma_to(af_t[0:1, o_ccb:o_ccb + 16], cc2o),
                         wait=[ev_cc2], sig="gd", sig_n=16)
        ev_s2s = kb.chain("vector", [
            lambda e: e.tensor_scalar(sc[0:1, o_misc + 8:o_misc + 9],
                                      af_t[0:1, o_ccb:o_ccb + 1],
                                      1e-8, 1.0 / 255.0, Alu.max, Alu.mult),
            lambda e: e.reciprocal(sc[0:1, o_misc + 9:o_misc + 10],
                                   sc[0:1, o_misc + 8:o_misc + 9]),
        ], wait=[ev_cr2])
        ev_bs2 = bcast_row(sc[:, o_misc + 9:o_misc + 10], sc[:, o_misc + 10:o_misc + 11],
                           1, wait=[ev_s2s])
        o_s2sb = AF.alloc(D)
        o_b2qb = AF.alloc(D)
        ev_mr2 = kb.chain("vector", [
            lambda e: e.tensor_scalar(af_t[0:1, o_sw2row:o_sw2row + D],
                                      af_t[0:1, o_sw2row:o_sw2row + D],
                                      sc[0:1, o_misc + 8:o_misc + 9], None, Alu.mult),
            lambda e: e.reciprocal(af_t[0:1, o_rb2:o_rb2 + D],
                                   af_t[0:1, o_sw2row:o_sw2row + D]),
        ], wait=[ev_s2s])
        ev_lb2 = kb.step("sync", dma_to(af_t[0:1, o_b2r:o_b2r + D], b2row),
                         sig="ld", sig_n=16)
        ev_b2q = kb.chain("vector", [
            lambda e: e.tensor_tensor(af_t[0:1, o_b2r:o_b2r + D],
                                      af_t[0:1, o_b2r:o_b2r + D],
                                      af_t[0:1, o_rb2:o_rb2 + D], Alu.mult),
            lambda e: e.tensor_scalar(af_t[0:1, o_b2r:o_b2r + D],
                                      af_t[0:1, o_b2r:o_b2r + D],
                                      MAGIC, MAGIC, Alu.add, Alu.subtract),
            lambda e: e.tensor_tensor(af_t[0:1, o_b2r:o_b2r + D],
                                      af_t[0:1, o_b2r:o_b2r + D],
                                      af_t[0:1, o_sw2row:o_sw2row + D], Alu.mult),
        ], wait=[ev_mr2, ev_lb2])
        bcast_row(af_t[:, o_sw2row:o_sw2row + D], A(o_s2sb, D), D, wait=[ev_b2q])
        ev_bb2 = bcast_row(af_t[:, o_b2r:o_b2r + D], A(o_b2qb, D), D, wait=[ev_b2q])

        o_hf2 = [AF.alloc(FC * 256) for _ in range(2)]
        if 2 * FC * 256 <= DC * NO:
            o_hq2 = [o_xqT, o_xqT + FC * 256]
        else:
            o_hq2 = [o_xqT, AB.alloc(FC * 256)]
        o_ot = AF.alloc(D)
        NPART = 4
        PFC = FC // NPART
        HSEG = FC * 256 // NPART
        ev_ldq = {}
        ev_qp = {}
        ev_mml = {}
        ev_qlast = {}

        def emit_hload(ts, part):
            b = ts % 2
            f0, f1 = part * PFC, (part + 1) * PFC
            def rdh(e, ts=ts, f0=f0, f1=f1, b=b):
                src2 = s_h.rearrange("p (fc no) -> p fc no", fc=FC)[:, f0:f1,
                                                                   ts * 256:(ts + 1) * 256]
                dst = A(o_hf2[b], FC * 256).rearrange("p (fc c) -> p fc c",
                                                      fc=FC)[:, f0:f1, :]
                return e.dma_start(out=dst, in_=src2)
            w = []
            if part == 0:
                w = ([kb.mark("sp"), kb.mark("V"), kb.mark("T")] if ts < 2
                     else [ev_qlast[ts - 2]])
            ev_ldq[(ts, part)] = kb.step("sync", rdh, wait=w, sig="ld", sig_n=16)

        def emit_hq(ts, part):
            b = ts % 2
            o0 = part * HSEG
            def qp(e, b=b, o0=o0):
                e.tensor_scalar(af_t[:, o_hf2[b] + o0:o_hf2[b] + o0 + HSEG],
                                af_t[:, o_hf2[b] + o0:o_hf2[b] + o0 + HSEG],
                                sc[:, o_misc + 10:o_misc + 11], MAGIC,
                                Alu.mult, Alu.add)
                return e.tensor_scalar(ab_t[:, o_hq2[b] + o0:o_hq2[b] + o0 + HSEG],
                                       af_t[:, o_hf2[b] + o0:o_hf2[b] + o0 + HSEG],
                                       MAGIC, 255.0, Alu.subtract, Alu.min)
            w = [ev_ldq[(ts, part)], ev_bs2]
            if part == 0:
                w.append(kb.mark("T") if ts < 2 else ev_mml[ts - 2])
            ev_qp[(ts, part)] = kb.step("vector", qp, wait=w, sig="V")
            if part == NPART - 1:
                ev_qlast[ts] = ev_qp[(ts, part)]

        gF = [0]
        levacF = [None, None]
        for part in range(NPART):
            emit_hload(0, part)
            emit_hq(0, part)
        for ts in range(TS2):
            b = ts % 2
            nxt = [] if ts + 1 >= TS2 else [(ts + 1, p) for p in range(NPART)]
            pf = [0]
            def pump_f(nxt=nxt, pf=pf):
                if pf[0] < len(nxt):
                    t2, p2 = nxt[pf[0]]
                    emit_hload(t2, p2)
                    emit_hq(t2, p2)
                    pf[0] += 1
            mlastF = None
            for tt in range(2):
                gt = ts * 2 + tt
                ev_lx2 = kb.step("sync", dma_to(A(o_x2t2, D), s_x2[:, gt * D:(gt + 1) * D]),
                                 wait=[kb.mark("V")], sig="ld", sig_n=16)
                def x2pb(e):
                    return e.tensor_tensor(A(o_x2t2, D), A(o_x2t2, D), A(o_b2qb, D), Alu.add)
                ev_xb = kb.step("vector", x2pb, wait=[ev_lx2, ev_bb2], sig="V")
                for doc in range(NFO):
                    half = gF[0] % 2
                    gF[0] += 1
                    pb = ps_a[:, half * 512:half * 512 + FOW]
                    for fcc in range(FC):
                        lhs = Ab(o_hq2[b] + fcc * 256 + tt * P, P)
                        rhs = Ab(o_w1ct + fcc * D + doc * FOW, FOW)
                        w = []
                        if fcc == 0:
                            w = [ev_qp[(ts, 0)]]
                            w.append(levacF[half] if gF[0] > 2 else kb.mark("V"))
                        elif fcc % PFC == 0:
                            w = [ev_qp[(ts, fcc // PFC)]]
                        if ts == 0:
                            w.append(ev_w2ld[fcc])
                        def mm(e, lhs=lhs, rhs=rhs, fcc=fcc, pb=pb):
                            return e.matmul(pb, lhs, rhs, start=(fcc == 0),
                                            stop=(fcc == FC - 1))
                        mlastF = kb.step("tensor", mm, wait=w, sig="T")
                    def evo(e, doc=doc, pb=pb):
                        e.tensor_tensor(A(o_ot + doc * FOW, FOW), pb,
                                        A(o_s2sb + doc * FOW, FOW), Alu.mult)
                        return e.tensor_tensor(A(o_ot + doc * FOW, FOW),
                                               A(o_ot + doc * FOW, FOW),
                                               A(o_x2t2 + doc * FOW, FOW), Alu.add)
                    levacF[half] = kb.step("vector", evo,
                                           wait=[mlastF, ev_xb, kb.mark("sp")], sig="V")
                    pump_f()
                ev_ot = kb.mark("V")
                kb.step("scalar", dma_to(out[:, gt * D:(gt + 1) * D], A(o_ot, D)),
                        wait=[ev_ot], sig="sp", sig_n=16)
            while pf[0] < len(nxt):
                pump_f()
            ev_mml[ts] = mlastF

        if dbg:
            dbg_list = [("d_qh", s_qh), ("d_ql", s_ql), ("d_kh", s_kh),
                        ("d_vt", s_vt), ("d_oh", s_oh), ("d_ol", s_ol),
                        ("d_x2", s_x2), ("d_h", s_h), ("d_w1c", s_w1c), ("d_w2c", s_w2c),
                        ("d_l1h", s_l1h), ("d_l1l", s_l1l)]
            dbg_list += [("d_cc1", cc1o), ("d_cc2", cc2o)]
            wall = [kb.mark("sp"), kb.mark("ld"), kb.mark("V"), kb.mark("S"),
                    kb.mark("T")]
            for nm, src in dbg_list:
                t = src.tensor
                d = nc.dram_tensor(nm, list(t.shape), t.dtype, kind="ExternalOutput").ap()
                kb.step("gpsimd", dma_to(d, src), wait=wall, sig="gd", sig_n=16)
            d_sc = nc.dram_tensor("d_sc", [P, 768], F32, kind="ExternalOutput").ap()
            kb.step("gpsimd", dma_to(d_sc, sc[:, 0:768]), wait=wall, sig="gd", sig_n=16)

        final_sp = kb.mark("sp")
        final_gd = kb.mark("gd")
        kb.step("gpsimd", lambda e: None, wait=[final_sp, final_gd])

        with nc.Block() as block:
            kb.emit(block)

    return nc


def shard_inputs(ins, cfg):
    D, NB, H, F = cfg["D"], cfg["NB"], cfg["H"], cfg["F"]
    DC, FC, MT = D // P, F // P, NB // P
    x = np.ascontiguousarray(ins["L0_attn_x"], dtype=np.float32)
    B_ = x.shape[0]

    def colmajor(w):
        Aq, B2 = w.shape
        return np.ascontiguousarray(
            w.reshape(Aq // P, P, B2).transpose(1, 0, 2).reshape(P, (Aq // P) * B2))

    common = dict(
        wqT=colmajor(np.ascontiguousarray(np.asarray(ins["L0_attn_Wq"], np.float32).T)),
        wkT=colmajor(np.ascontiguousarray(np.asarray(ins["L0_attn_Wk"], np.float32).T)),
        wvT=colmajor(np.ascontiguousarray(np.asarray(ins["L0_attn_Wv"], np.float32).T)),
        woT=colmajor(np.ascontiguousarray(np.asarray(ins["L0_attn_Wo"], np.float32).T)),
        w1n=colmajor(np.asarray(ins["L0_ffn_w1"], np.float32)),
        w1t=colmajor(np.ascontiguousarray(np.asarray(ins["L0_ffn_w1"], np.float32).T)),
        w2n=colmajor(np.asarray(ins["L0_ffn_w2"], np.float32)),
        w2t=colmajor(np.ascontiguousarray(np.asarray(ins["L0_ffn_w2"], np.float32).T)),
        bqc=np.ascontiguousarray(np.asarray(ins["L0_attn_bq"], np.float32).reshape(DC, P).T),
        bkc=np.ascontiguousarray(np.asarray(ins["L0_attn_bk"], np.float32).reshape(DC, P).T),
        bvc=np.ascontiguousarray(np.asarray(ins["L0_attn_bv"], np.float32).reshape(DC, P).T),
        borow=np.ascontiguousarray(np.asarray(ins["L0_attn_bo"], np.float32).reshape(1, D)),
        b1c=np.ascontiguousarray(np.asarray(ins["L0_ffn_b1"], np.float32).reshape(FC, P).T),
        b2row=np.ascontiguousarray(np.asarray(ins["L0_ffn_b2"], np.float32).reshape(1, D)),
    )
    in_maps = []
    for c in range(8):
        b, h = c // 2, c % 2
        xb = x[b]
        xr = np.concatenate([xb[h * (NB // 2):(h + 1) * (NB // 2)],
                             xb[(1 - h) * (NB // 2):(2 - h) * (NB // 2)]], axis=0)
        xin = np.ascontiguousarray(
            xr.reshape(MT, P, D).transpose(1, 0, 2).reshape(P, MT * D))
        m = dict(common)
        m["xin"] = xin
        in_maps.append(m)
    return in_maps


def kernel(**inputs):
    cfg = FULL_CFG
    D, NB = cfg["D"], cfg["NB"]
    NO, TT = NB // 2, NB // 2 // P
    nc = build_nc(cfg)
    in_maps = shard_inputs(inputs, cfg)
    res = run_bass_kernel_spmd(nc, in_maps, core_ids=list(range(8)))
    B_ = inputs["L0_attn_x"].shape[0]
    outv = np.zeros((B_, NB, D), dtype=np.float32)
    for c in range(8):
        b, h = c // 2, c % 2
        o = np.asarray(res.results[c]["out"])
        o = o.reshape(P, TT, D).transpose(1, 0, 2).reshape(NO, D)
        outv[b, h * NO:(h + 1) * NO] = o
    return outv


# version 17
# speedup vs baseline: 1.0113x; 1.0113x over previous
import contextlib
import numpy as np
import concourse.bass as bass
import concourse.mybir as mybir
from concourse.bass_utils import run_bass_kernel_spmd

F32 = mybir.dt.float32
BF16 = mybir.dt.bfloat16
Alu = mybir.AluOpType
Act = mybir.ActivationFunctionType
AX = mybir.AxisListType.X
P = 128
MAGIC = 12582912.0

FULL_CFG = dict(D=1024, NB=4096, H=16, F=4096)

ENGS = ("sync", "scalar", "vector", "tensor", "gpsimd")


class KB:

    STEP_LIMIT = None

    def __init__(self, nc):
        self.nc = nc
        self.steps = {e: [] for e in ENGS}
        self.counts = {}
        self.sem_handles = {}
        self.seq = 0

    def step(self, eng, fn, wait=(), sig=None, sig_n=1):
        self.seq += 1
        if KB.STEP_LIMIT is not None and self.seq > KB.STEP_LIMIT:
            if sig is not None:
                v = self.counts.get(sig, 0) + sig_n
                self.counts[sig] = v
                return (sig, v)
            return None
        w = tuple(x for x in wait if x is not None)
        self.steps[eng].append((fn, w, sig, sig_n))
        if sig is not None:
            v = self.counts.get(sig, 0) + sig_n
            self.counts[sig] = v
            return (sig, v)
        return None

    def mark(self, sig):
        return (sig, self.counts.get(sig, 0))

    def chain(self, eng, fns, wait=(), sig=None):
        if sig is None:
            sig = {"vector": "V", "scalar": "S", "tensor": "T", "gpsimd": "G"}[eng]
        ev = None
        for i, f in enumerate(fns):
            w = (list(wait) if i == 0 else []) + ([ev] if ev is not None else [])
            ev = self.step(eng, f, wait=w, sig=sig)
        return ev

    def emit(self, block):
        waited = {e: {} for e in ENGS}

        def mk(eng):
            def body(engine):
                for fn, waits, sig, sig_n in self.steps[eng]:
                    for (s, v) in waits:
                        if v > 0 and waited[eng].get(s, 0) < v:
                            engine.wait_ge(self.sem_handles[s], v)
                            waited[eng][s] = v
                    inst = fn(engine)
                    if sig is not None:
                        assert inst is not None, f"{eng} step signalling {sig} returned None"
                        inst.then_inc(self.sem_handles[sig], sig_n)
            return body

        for eng in ENGS:
            getattr(block, eng)(mk(eng))


class Arena:
    def __init__(self, cols):
        self.cols = cols
        self.off = 0
        self.saved = []
        self.hw = 0

    def alloc(self, n):
        o = self.off
        assert o + n <= self.cols, f"arena overflow: {o}+{n} > {self.cols}"
        self.off = o + n
        self.hw = max(self.hw, self.off)
        return o

    def push(self):
        self.saved.append(self.off)

    def pop(self):
        self.off = self.saved.pop()


def build_nc(cfg, dbg=False):
    D, NB, H, F = cfg["D"], cfg["NB"], cfg["H"], cfg["F"]
    NO = NB // 2
    DC = D // P
    FC = F // P
    MT = NB // P
    TT = NO // P
    HPn = H // 2
    assert HPn == DC, "head-pair count must equal d-chunk count for the spill layout"
    NCH = NO // 512
    MC1 = NB // 512
    TS2 = NO // 256
    WT = HPn * 130
    NFO = max(1, D // 512)
    FOW = D // NFO

    nc = bass.Bass("TRN2", target_bir_lowering=False, detect_race_conditions=False,
                   dynamic_dma_scratch_size=8192)

    xin = nc.dram_tensor("xin", [P, MT * D], F32, kind="ExternalInput").ap()
    wqT = nc.dram_tensor("wqT", [P, DC * D], F32, kind="ExternalInput").ap()
    wkT = nc.dram_tensor("wkT", [P, DC * D], F32, kind="ExternalInput").ap()
    wvT = nc.dram_tensor("wvT", [P, DC * D], F32, kind="ExternalInput").ap()
    woT = nc.dram_tensor("woT", [P, DC * D], F32, kind="ExternalInput").ap()
    bqc = nc.dram_tensor("bqc", [P, DC], F32, kind="ExternalInput").ap()
    bkc = nc.dram_tensor("bkc", [P, DC], F32, kind="ExternalInput").ap()
    bvc = nc.dram_tensor("bvc", [P, DC], F32, kind="ExternalInput").ap()
    borow = nc.dram_tensor("borow", [1, D], F32, kind="ExternalInput").ap()
    w1n = nc.dram_tensor("w1n", [P, FC * D], F32, kind="ExternalInput").ap()
    w1t = nc.dram_tensor("w1t", [P, DC * F], F32, kind="ExternalInput").ap()
    b1c = nc.dram_tensor("b1c", [P, FC], F32, kind="ExternalInput").ap()
    w2n = nc.dram_tensor("w2n", [P, DC * F], F32, kind="ExternalInput").ap()
    w2t = nc.dram_tensor("w2t", [P, FC * D], F32, kind="ExternalInput").ap()
    b2row = nc.dram_tensor("b2row", [1, D], F32, kind="ExternalInput").ap()
    out = nc.dram_tensor("out", [P, TT * D], F32, kind="ExternalOutput").ap()

    s_l1h = nc.dram_tensor("s_l1h", [P, DC * NB], BF16).ap()
    s_l1l = nc.dram_tensor("s_l1l", [P, DC * NB], BF16).ap()
    s_qh = nc.dram_tensor("s_qh", [P, DC * NB], BF16).ap()
    s_ql = nc.dram_tensor("s_ql", [P, DC * NB], BF16).ap()
    s_kh = nc.dram_tensor("s_kh", [P, DC * NO], BF16).ap()
    s_kl = nc.dram_tensor("s_kl", [P, DC * NO], BF16).ap()
    s_vt = nc.dram_tensor("s_vt", [P, HPn * MT * 130], F32).ap()
    s_oh = nc.dram_tensor("s_oh", [64, HPn * 2 * NO], BF16).ap()
    s_ol = nc.dram_tensor("s_ol", [64, HPn * 2 * NO], BF16).ap()
    s_x2 = nc.dram_tensor("s_x2", [P, TT * D], F32).ap()
    s_h = nc.dram_tensor("s_h", [P, FC * NO], F32).ap()
    s_w1c = nc.dram_tensor("s_w1c", [P, DC * F], BF16).ap()
    s_w2c = nc.dram_tensor("s_w2c", [P, FC * D], BF16).ap()
    cc1i = nc.dram_tensor("cc1i", [1, 16], F32).ap()
    cc1o = nc.dram_tensor("cc1o", [1, 16], F32).ap()
    cc2i = nc.dram_tensor("cc2i", [1, 16], F32).ap()
    cc2o = nc.dram_tensor("cc2o", [1, 16], F32).ap()

    kb = KB(nc)

    with contextlib.ExitStack() as stack:
        ent = stack.enter_context
        base_af = 2 * P + 16 + 3 * D + 64
        AFN = base_af + max(DC * D + 9 * D,
                            2 * MT * 130 + 2 * 1024 + 2 * NO + 2112 + 64
                            + 4 * D,
                            DC * D + 6 * D,
                            5 * D + NO + 2 * FC * 256)
        ABN = P + max(4 * DC * D + 4 * DC * 512 + 4 * 512,
                      2 * DC * D + 2 * HPn * NO,
                      4 * NB + 4 * NO + NB + 4 * NO + 1 + 8 + 2 * D,
                      DC * NO + D + DC * F)
        af_t = ent(nc.sbuf_tensor("AF", [P, AFN], F32))
        ab_t = ent(nc.sbuf_tensor("AB", [P, ABN], BF16))
        sc_t = ent(nc.sbuf_tensor("SC", [P, 768], F32))
        AF = Arena(AFN)
        AB = Arena(ABN)
        SC = Arena(768)
        sc = sc_t

        ps_a = ent(nc.psum_tensor("ps_a", [P, 1024], F32))
        ps_b = ent(nc.psum_tensor("ps_b", [P, 1024], F32))
        ps_pv = ent(nc.psum_tensor("ps_pv", [P, 1024], F32))
        ps_x = ent(nc.psum_tensor("ps_x", [P, 512], F32))
        ps_xb = ent(nc.psum_tensor("ps_xb", [P, 1024], BF16))

        for s in ("ld", "sp", "gd", "T", "V", "S", "G", "cc"):
            kb.sem_handles[s] = ent(nc.semaphore(s))

        def A(off, n):
            return af_t[:, off:off + n]

        def Ab(off, n):
            return ab_t[:, off:off + n]

        o_stats = SC.alloc(64)
        o_mu2 = SC.alloc(TT)
        o_r2 = SC.alloc(TT)
        o_sw1 = SC.alloc(FC)
        o_sw2 = SC.alloc(DC)
        o_b1q = SC.alloc(FC)
        o_sc1 = SC.alloc(FC)
        o_misc = SC.alloc(32)
        o_negc = SC.alloc(2)
        o_lmax = SC.alloc(8)
        o_bq = SC.alloc(DC)
        o_bk = SC.alloc(DC)
        o_bv = SC.alloc(DC)

        def dma_to(dst, src):
            def fn(e):
                return e.dma_start(out=dst, in_=src)
            return fn

        o_idf = AF.alloc(P)
        o_ccb = AF.alloc(16)
        o_sw2row = AF.alloc(D)
        o_rb2 = AF.alloc(D)
        o_b2r = AF.alloc(D)
        o_idb = AB.alloc(P)

        ev_b = kb.step("sync", dma_to(sc[:, o_bq:o_bq + DC], bqc), sig="ld", sig_n=16)
        kb.step("sync", dma_to(sc[:, o_bk:o_bk + DC], bkc), sig="ld", sig_n=16)
        ev_bv_ = kb.step("sync", dma_to(sc[:, o_bv:o_bv + DC], bvc), sig="ld", sig_n=16)
        ev_b1c = kb.step("sync", dma_to(sc[:, o_b1q:o_b1q + FC], b1c), sig="ld", sig_n=16)

        def mk_ident(e):
            e.memset(A(o_idf, P), 0.0)
            e.affine_select(out=A(o_idf, P), in_=A(o_idf, P),
                            compare_op=Alu.not_equal, fill=1.0,
                            base=0, pattern=[[-1, P]], channel_multiplier=1)
            e.memset(Ab(o_idb, P), 0.0)
            return e.affine_select(out=Ab(o_idb, P), in_=Ab(o_idb, P),
                                   compare_op=Alu.not_equal, fill=1.0,
                                   base=0, pattern=[[-1, P]], channel_multiplier=1)
        o_ones0 = AF.alloc(P)
        def mk_ones0(e):
            return e.memset(af_t[0:1, o_ones0:o_ones0 + P], 1.0)
        ev_o0 = kb.step("vector", mk_ones0, sig="V")

        ev_id = kb.step("gpsimd", mk_ident, sig="G")

        def bcast_row(src_ap, dst_ap, ncols, wait=()):
            ev = None
            done = 0
            while done < ncols:
                w = min(512, ncols - done)
                def mmb(e, done=done, w=w):
                    return e.matmul(ps_x[:, 0:w], af_t[0:1, o_ones0:o_ones0 + P],
                                    src_ap[0:1, done:done + w], start=True, stop=True)
                ev_m = kb.step("tensor", mmb,
                               wait=list(wait) + [ev_o0, kb.mark("V")], sig="T")
                def cpb(e, done=done, w=w):
                    return e.tensor_copy(dst_ap[:, done:done + w], ps_x[:, 0:w])
                ev = kb.step("vector", cpb, wait=[ev_m], sig="V")
                done += w
            return ev

        AF.push(); AB.push()
        o_wstage = AF.alloc(DC * D)
        o_wah = AB.alloc(DC * D); o_wal = AB.alloc(DC * D)
        o_wbh = AB.alloc(DC * D); o_wbl = AB.alloc(DC * D)

        def split_weight(src_dram, dst_hi, dst_lo, prev_ev, stage):
            ev_l = kb.step("sync", dma_to(A(stage, DC * D), src_dram),
                           wait=[prev_ev, kb.mark("V")], sig="ld", sig_n=16)
            def hi(e):
                return e.tensor_copy(Ab(dst_hi, DC * D), A(stage, DC * D))
            kb.step("vector", hi, wait=[ev_l, kb.mark("T")], sig="V")
            def lo(e):
                return e.scalar_tensor_tensor(Ab(dst_lo, DC * D), A(stage, DC * D), 0.0,
                                              Ab(dst_hi, DC * D), Alu.add, Alu.subtract)
            return kb.step("vector", lo, sig="V")

        ev_wq = split_weight(wqT, o_wah, o_wal, None, o_wstage)
        ev_wk = split_weight(wkT, o_wbh, o_wbl, ev_wq, o_wstage)

        o_lnt = AF.alloc(4 * D)
        o_xc0 = AF.alloc(4 * D)
        o_xc1 = AF.alloc(4 * D)
        o_sq = AF.alloc(D)
        o_xc = [o_xc0, o_xc1]
        o_vf = o_lnt
        o_vt = o_xc0
        assert 4 * WT <= 8 * D, "V token chunk alias overflow"
        o_l1h = [AB.alloc(DC * 512) for _ in range(2)]
        o_l1l = [AB.alloc(DC * 512) for _ in range(2)]
        o_eh = AB.alloc(1024); o_el = AB.alloc(1024)

        ev_trpv = [None, None, None, None]
        ev_redpv = [None]
        ev_elopv = [None]

        def ln1_chunk(mc, buf, wait_extra=()):
            ev_x = kb.step("sync", dma_to(A(o_xc[buf], 4 * D),
                                          xin[:, mc * 4 * D:(mc + 1) * 4 * D]),
                           wait=list(wait_extra), sig="ld", sig_n=16)
            ev_e = None
            for tt in range(4):
                xs = A(o_xc[buf] + tt * D, D)
                def sq(e, xs=xs):
                    return e.activation(A(o_sq, D), xs, Act.Square)
                ev_sq = kb.step("scalar", sq, wait=[ev_x, ev_redpv[0]], sig="S")
                s1 = sc[:, o_stats:o_stats + 1]
                s2 = sc[:, o_stats + 1:o_stats + 2]
                mu = sc[:, o_stats + 2:o_stats + 3]
                rr = sc[:, o_stats + 3:o_stats + 4]
                rr2 = sc[:, o_stats + 4:o_stats + 5]
                ev_red = kb.chain("vector", [
                    lambda e, xs=xs, s1=s1: (e.reduce_sum(s1, xs, AX),
                                             e.reduce_sum(s2, A(o_sq, D), AX))[1],
                    lambda e, s1=s1, mu=mu: (e.tensor_scalar(mu, s1, 1.0 / D, None, Alu.mult),
                                             e.tensor_scalar(s2, s2, 1.0 / D, None, Alu.mult))[1],
                    lambda e, mu=mu, s1=s1: e.tensor_tensor(s1, mu, mu, Alu.mult),
                    lambda e, s1=s1, s2=s2: e.tensor_tensor(s2, s2, s1, Alu.subtract),
                    lambda e, s2=s2: e.tensor_scalar(s2, s2, 1e-5, None, Alu.add),
                ], wait=[ev_x, ev_sq])
                ev_redpv[0] = ev_red
                def sq2(e, s2=s2, rr=rr):
                    return e.activation(rr, s2, Act.Sqrt)
                ev_rt = kb.step("scalar", sq2, wait=[ev_red], sig="S")
                ev_ln = kb.chain("vector", [
                    lambda e, rr=rr, rr2=rr2: e.reciprocal(rr2, rr),
                    lambda e, xs=xs, mu=mu, rr2=rr2, tt=tt: e.tensor_scalar(
                        A(o_lnt + tt * D, D), xs, mu, rr2, Alu.subtract, Alu.mult),
                ], wait=[ev_rt, ev_trpv[tt]])
                for dc in range(DC):
                    def tr(e, tt=tt, dc=dc):
                        return e.transpose(ps_x[:, 0:P], A(o_lnt + tt * D + dc * P, P),
                                           A(o_idf, P))
                    ev_t = kb.step("tensor", tr,
                                   wait=[ev_ln, ev_id, ev_elopv[0]], sig="T")
                    ev_trpv[tt] = ev_t
                    def ehi(e, tt=tt, dc=dc, buf=buf):
                        return e.tensor_copy(Ab(o_l1h[buf] + dc * 512 + tt * P, P),
                                             ps_x[:, 0:P])
                    kb.step("vector", ehi, wait=[ev_t], sig="V")
                    def elo(e, tt=tt, dc=dc, buf=buf):
                        return e.scalar_tensor_tensor(Ab(o_l1l[buf] + dc * 512 + tt * P, P),
                                                      ps_x[:, 0:P], 0.0,
                                                      Ab(o_l1h[buf] + dc * 512 + tt * P, P),
                                                      Alu.add, Alu.subtract)
                    ev_e = kb.step("vector", elo, sig="V")
                    ev_elopv[0] = ev_e
            def sp_h(e, mc=mc, buf=buf):
                return e.dma_start(
                    out=s_l1h.rearrange("p (dc nb) -> p dc nb", dc=DC)[:, :, mc * 512:(mc + 1) * 512],
                    in_=Ab(o_l1h[buf], DC * 512).rearrange("p (dc m) -> p dc m", dc=DC))
            kb.step("scalar", sp_h, wait=[ev_e], sig="sp", sig_n=16)
            def sp_l(e, mc=mc, buf=buf):
                return e.dma_start(
                    out=s_l1l.rearrange("p (dc nb) -> p dc nb", dc=DC)[:, :, mc * 512:(mc + 1) * 512],
                    in_=Ab(o_l1l[buf], DC * 512).rearrange("p (dc m) -> p dc m", dc=DC))
            kb.step("scalar", sp_l, sig="sp", sig_n=16)
            return ev_e

        ev_qmm = {}

        def proj_block(buf, whi_o, wlo_o, bias_o, psum, dst_h, dst_l, col0, n_cols_dst,
                       first_wait, qkey=None):
            levac = [None, None]
            lspill = [None, None]
            for fc in range(DC):
                half = fc % 2
                pb = psum[:, half * 512:half * 512 + 512]
                mlast = None
                for dc in range(DC):
                    lnh = Ab(o_l1h[buf] + dc * 512, 512)
                    lnl = Ab(o_l1l[buf] + dc * 512, 512)
                    wh = Ab(whi_o + dc * D + fc * P, P)
                    wl = Ab(wlo_o + dc * D + fc * P, P)
                    w = []
                    if dc == 0:
                        w = list(first_wait) if fc == 0 else []
                        w.append(levac[half])
                    def mm(e, wh=wh, wl=wl, lnh=lnh, lnl=lnl, dc=dc, pb=pb):
                        e.matmul(pb, wh, lnh, start=(dc == 0), stop=False)
                        e.matmul(pb, wh, lnl, start=False, stop=False)
                        return e.matmul(pb, wl, lnh, start=False, stop=(dc == DC - 1))
                    mlast = kb.step("tensor", mm, wait=w, sig="T")
                if qkey is not None:
                    ev_qmm[qkey] = mlast
                bias = sc[:, bias_o + fc:bias_o + fc + 1]
                eh_o = o_eh + half * 512
                el_o = o_el + half * 512
                def ehi(e, bias=bias, pb=pb, eh_o=eh_o):
                    return e.tensor_scalar(Ab(eh_o, 512), pb, bias, None, Alu.add)
                kb.step("vector", ehi, wait=[mlast, ev_b1c, lspill[half]], sig="V")
                def elo(e, bias=bias, pb=pb, eh_o=eh_o, el_o=el_o):
                    return e.scalar_tensor_tensor(Ab(el_o, 512), pb, bias,
                                                  Ab(eh_o, 512), Alu.add, Alu.subtract)
                ev_e = kb.step("vector", elo, sig="V")
                levac[half] = ev_e
                th = dst_h.rearrange("p (dc nb) -> p dc nb", dc=DC)
                tl = dst_l.rearrange("p (dc nb) -> p dc nb", dc=DC)
                def sph(e, fc=fc, th=th, col0=col0, eh_o=eh_o):
                    return e.dma_start(out=th[:, fc, col0:col0 + 512], in_=Ab(eh_o, 512))
                kb.step("scalar", sph, wait=[ev_e], sig="sp", sig_n=16)
                def spl(e, fc=fc, tl=tl, col0=col0, el_o=el_o):
                    return e.dma_start(out=tl[:, fc, col0:col0 + 512], in_=Ab(el_o, 512))
                lspill[half] = kb.step("scalar", spl, sig="sp", sig_n=16)

        ev_lnc = {}
        for mc in range(MC1):
            buf = mc % 2
            wx = [ev_lnc[mc - 2]] if mc >= 2 else []
            ev_ln = ln1_chunk(mc, buf, wait_extra=wx)
            ev_lnc[mc] = ev_ln
            proj_block(buf, o_wah, o_wal, o_bq, ps_a, s_qh, s_ql, mc * 512, NB,
                       [ev_ln, ev_wq], qkey=mc)
            if mc < MC1 // 2:
                proj_block(buf, o_wbh, o_wbl, o_bk, ps_b, s_kh, s_kl, mc * 512, NO,
                           [ev_ln, ev_wk])

        evA1_T = kb.mark("T"); evA1_sp = kb.mark("sp")

        ev_wv = split_weight(wvT, o_wah, o_wal, None, o_wstage)

        ev_vmm = {}
        for mc in range(MC1):
            buf = mc % 2
            def rl_h(e, mc=mc, buf=buf):
                return e.dma_start(
                    out=Ab(o_l1h[buf], DC * 512).rearrange("p (dc m) -> p dc m", dc=DC),
                    in_=s_l1h.rearrange("p (dc nb) -> p dc nb", dc=DC)[:, :, mc * 512:(mc + 1) * 512])
            wrl = [evA1_sp, ev_qmm[MC1 - 2 + mc]] if mc < 2 else [ev_vmm[mc - 2]]
            kb.step("sync", rl_h, wait=wrl, sig="ld", sig_n=16)
            def rl_l(e, mc=mc, buf=buf):
                return e.dma_start(
                    out=Ab(o_l1l[buf], DC * 512).rearrange("p (dc m) -> p dc m", dc=DC),
                    in_=s_l1l.rearrange("p (dc nb) -> p dc nb", dc=DC)[:, :, mc * 512:(mc + 1) * 512])
            ev_rl = kb.step("sync", rl_l, sig="ld", sig_n=16)
            levacV = [None, None]
            for fc in range(DC):
                half = fc % 2
                pb = ps_a[:, half * 512:half * 512 + 512]
                mlast = None
                for dc in range(DC):
                    lnh = Ab(o_l1h[buf] + dc * 512, 512)
                    lnl = Ab(o_l1l[buf] + dc * 512, 512)
                    wh = Ab(o_wah + dc * D + fc * P, P)
                    wl = Ab(o_wal + dc * D + fc * P, P)
                    w = []
                    if dc == 0:
                        w = [ev_rl, ev_wv] if fc == 0 else []
                        w.append(levacV[half])
                        if fc == 0:
                            w.append(kb.mark("V"))
                    def mm(e, wh=wh, wl=wl, lnh=lnh, lnl=lnl, dc=dc, pb=pb):
                        e.matmul(pb, wh, lnh, start=(dc == 0), stop=False)
                        e.matmul(pb, wh, lnl, start=False, stop=False)
                        return e.matmul(pb, wl, lnh, start=False, stop=(dc == DC - 1))
                    mlast = kb.step("tensor", mm, wait=w, sig="T")
                ev_vmm[mc] = mlast
                bias = sc[:, o_bv + fc:o_bv + fc + 1]
                def evf(e, bias=bias, fc=fc, pb=pb):
                    return e.tensor_scalar(A(o_vf + fc * 512, 512), pb,
                                           bias, None, Alu.add)
                levacV[half] = kb.step("vector", evf, wait=[mlast, ev_bv_], sig="V")
            ev_vf = kb.mark("V")
            for tt in range(4):
                mt = mc * 4 + tt
                ev_e2 = None
                for fc in range(DC):
                    def tr(e, fc=fc, tt=tt):
                        return e.transpose(ps_x[:, 0:P], A(o_vf + fc * 512 + tt * P, P),
                                           A(o_idf, P))
                    ev_t = kb.step("tensor", tr, wait=[ev_vf, kb.mark("V")], sig="T")
                    def evc(e, fc=fc, tt=tt):
                        dst = A(o_vt + tt * WT, WT).rearrange("p (hh c) -> p hh c", c=65)
                        return e.tensor_copy(dst[:, 2 * fc:2 * fc + 2, 0:64],
                                             ps_x[:, 0:P].rearrange("p (hh c) -> p hh c", c=64))
                    ev_e2 = kb.step("vector", evc, wait=[ev_t], sig="V")
                def ones_(e, tt=tt):
                    dst = A(o_vt + tt * WT, WT).rearrange("p (hh c) -> p hh c", c=65)
                    return e.memset(dst[:, :, 64:65], 1.0)
                ev_o = kb.step("vector", ones_, wait=[ev_e2], sig="V")
                def spv(e, mt=mt, tt=tt):
                    dst = s_vt.rearrange("p (hp mt c) -> p hp mt c", hp=HPn, mt=MT)
                    src = A(o_vt + tt * WT, WT).rearrange("p (hp c) -> p hp c", c=130)
                    return e.dma_start(out=dst[:, :, mt, :], in_=src)
                kb.step("scalar", spv, wait=[ev_o], sig="sp", sig_n=16)

        evA_T = kb.mark("T"); evA_sp = kb.mark("sp"); evA_V = kb.mark("V")
        AF.pop(); AB.pop()

        AF.push(); AB.push()
        o_vtk2 = [AF.alloc(MT * 130) for _ in range(2)]
        o_pb = [AF.alloc(1024) for _ in range(2)]
        o_ohp = AF.alloc(2 * NO)
        o_rb = AF.alloc(1024)
        o_rbs = AF.alloc(1024)
        o_ones64 = AF.alloc(64)
        o_wn2 = [AF.alloc(D) for _ in range(2)]
        o_wrbc = AF.alloc(D)
        o_wrow = AF.alloc(D)
        o_qhi2 = [AB.alloc(NB) for _ in range(2)]
        o_qlo2 = [AB.alloc(NB) for _ in range(2)]
        o_khi2 = [AB.alloc(NO) for _ in range(2)]
        o_klo2 = [AB.alloc(NO) for _ in range(2)]
        o_q2 = AB.alloc(NB)
        o_oh = AB.alloc(2 * NO); o_ol = AB.alloc(2 * NO)
        o_onesb = AB.alloc(1)
        o_wcod2 = [AB.alloc(D) for _ in range(2)]
        def mkones(e):
            e.memset(Ab(o_onesb, 1), 1.0)
            return e.memset(af_t[64:65, o_ones64:o_ones64 + 64], 1.0)
        ev_ones = kb.step("vector", mkones, wait=[evA_V, evA_T], sig="V")

        m2 = sc[0:1, o_misc:o_misc + 1]
        mprod = [sc[0:1, o_misc + 1 + i:o_misc + 2 + i] for i in range(4)]
        t0 = sc[0:1, o_misc + 5:o_misc + 6]
        ld_ev = {}
        cb_ev = {}

        def emit_loads(hp, wait0):
            b = hp % 2
            ev_lq = kb.step("sync", dma_to(Ab(o_qhi2[b], NB), s_qh[:, hp * NB:(hp + 1) * NB]),
                            wait=wait0, sig="ld", sig_n=16)
            kb.step("sync", dma_to(Ab(o_qlo2[b], NB), s_ql[:, hp * NB:(hp + 1) * NB]),
                    sig="ld", sig_n=16)
            kb.step("sync", dma_to(Ab(o_khi2[b], NO), s_kh[:, hp * NO:(hp + 1) * NO]),
                    sig="ld", sig_n=16)
            ev_lk = kb.step("sync", dma_to(Ab(o_klo2[b], NO), s_kl[:, hp * NO:(hp + 1) * NO]),
                            sig="ld", sig_n=16)
            ev_lv = kb.step("sync", dma_to(A(o_vtk2[b], MT * 130),
                                           s_vt[:, hp * MT * 130:(hp + 1) * MT * 130]),
                            sig="ld", sig_n=16)
            ld_ev[hp] = (ev_lq, ev_lk, ev_lv)

        def emit_norm_max(slot, ncols, head, ev_src):
            first = True
            for chunk in range(ncols // 512):
                def omm(e, chunk=chunk, head=head):
                    return e.matmul(ps_x[0:1, 0:512],
                                    Ab(o_onesb, 1)[head * 64:head * 64 + 64, 0:1],
                                    Ab(o_q2 + chunk * 512, 512)[head * 64:head * 64 + 64, :],
                                    start=True, stop=True,
                                    tile_position=(head * 64, 0))
                ev_m = kb.step("tensor", omm, wait=[ev_src, ev_ones, kb.mark("V")],
                               sig="T")
                if first:
                    kb.chain("vector", [
                        lambda e, slot=slot: e.reduce_max(slot, ps_x[0:1, 0:512], AX),
                    ], wait=[ev_m])
                else:
                    kb.chain("vector", [
                        lambda e: e.reduce_max(t0, ps_x[0:1, 0:512], AX),
                        lambda e, slot=slot: e.tensor_tensor(slot, slot, t0, Alu.max),
                    ], wait=[ev_m])
                first = False

        def emit_cbias(hp):
            b = hp % 2
            ev_lq, ev_lk, ev_lv = ld_ev[hp]
            def sqq(e, b=b):
                return e.tensor_tensor(Ab(o_q2, NB), Ab(o_qhi2[b], NB),
                                       Ab(o_qhi2[b], NB), Alu.mult)
            ev_s = kb.step("gpsimd", sqq, wait=[ev_lq, kb.mark("V")], sig="G")
            emit_norm_max(mprod[0], NB, 0, ev_s)
            emit_norm_max(mprod[1], NB, 1, ev_s)
            def sqk(e, b=b):
                return e.tensor_tensor(Ab(o_q2, NO), Ab(o_khi2[b], NO),
                                       Ab(o_khi2[b], NO), Alu.mult)
            ev_sk = kb.step("gpsimd", sqk, wait=[ev_lk, kb.mark("V")], sig="G")
            emit_norm_max(mprod[2], NO, 0, ev_sk)
            emit_norm_max(mprod[3], NO, 1, ev_sk)
            ev_c1 = kb.chain("vector", [
                lambda e: e.tensor_tensor(mprod[0], mprod[0], mprod[2], Alu.mult),
                lambda e: e.tensor_tensor(mprod[1], mprod[1], mprod[3], Alu.mult),
                lambda e: e.tensor_tensor(m2, mprod[0], mprod[1], Alu.max),
            ])
            def csqrt(e):
                return e.activation(m2, m2, Act.Sqrt)
            ev_c2 = kb.step("scalar", csqrt, wait=[ev_c1], sig="S")
            def cneg(e):
                return e.tensor_scalar(m2, m2, -1.02 / 32.0, None, Alu.mult)
            ev_c3 = kb.step("vector", cneg, wait=[ev_c2], sig="V")
            cb_ev[hp] = bcast_row(sc[:, o_misc:o_misc + 1],
                                  sc[:, o_negc + b:o_negc + b + 1], 1, wait=[ev_c3])


        JP = D // P
        wn_ev = [[evA_sp, evA_T, evA_V], [evA_sp, evA_T, evA_V]]
        wcod_ev = [[evA_T, evA_sp], [evA_T, evA_sp]]
        wsh = {}
        W_TASKS = []
        wtmp = sc[:, o_misc + 6:o_misc + 7]

        def t_w1max(fcc, bi):
            def task():
                ev_l = kb.step("sync",
                               dma_to(A(o_wn2[bi], D), w1n[:, fcc * D:(fcc + 1) * D]),
                               wait=wn_ev[bi], sig="ld", sig_n=16)
                def rmax(e, fcc=fcc, bi=bi):
                    return e.reduce_max(sc[:, o_sw1 + fcc:o_sw1 + fcc + 1],
                                        A(o_wn2[bi], D), AX, apply_absolute_value=True)
                wn_ev[bi] = [kb.step("vector", rmax, wait=[ev_l], sig="V")]
            return task

        def t_swfin(key, off, n):
            def task():
                def fin(e):
                    return e.tensor_scalar(sc[:, off:off + n], sc[:, off:off + n],
                                           1e-8, 1.0 / 127.0, Alu.max, Alu.mult)
                wsh[key] = kb.step("vector", fin, wait=wn_ev[0] + wn_ev[1], sig="V")
            return task

        def t_w1row(p):
            def task():
                for j in range(JP):
                    fcc = p * JP + j
                    def trs(e, fcc=fcc):
                        return e.transpose(ps_x[0:1, 0:P],
                                           sc[:, o_sw1 + fcc:o_sw1 + fcc + 1],
                                           A(o_idf, P))
                    ev_t = kb.step("tensor", trs, wait=[wsh["sw1"], kb.mark("V")],
                                   sig="T")
                    def cps(e, j=j):
                        return e.tensor_copy(
                            af_t[0:1, o_wrow + j * P:o_wrow + (j + 1) * P],
                            ps_x[0:1, 0:P])
                    kb.step("vector", cps, wait=[ev_t], sig="V")
                def rrow(e):
                    return e.reciprocal(af_t[0:1, o_wrow:o_wrow + D],
                                        af_t[0:1, o_wrow:o_wrow + D])
                ev_rr = kb.step("vector", rrow, wait=[kb.mark("V")], sig="V")
                wsh["bc"] = bcast_row(af_t[:, o_wrow:o_wrow + D], A(o_wrbc, D), D,
                                      wait=[ev_rr])
            return task

        def t_w2max(doc, p, bi):
            def task():
                ev_l = kb.step("sync",
                               dma_to(A(o_wn2[bi], D),
                                      w2n[:, doc * F + p * D:doc * F + (p + 1) * D]),
                               wait=wn_ev[bi], sig="ld", sig_n=16)
                dst = sc[:, o_sw2 + doc:o_sw2 + doc + 1]
                if p == 0:
                    def rmax(e, bi=bi, dst=dst):
                        return e.reduce_max(dst, A(o_wn2[bi], D), AX,
                                            apply_absolute_value=True)
                    wn_ev[bi] = [kb.step("vector", rmax, wait=[ev_l], sig="V")]
                else:
                    ev = kb.chain("vector", [
                        lambda e, bi=bi: e.reduce_max(wtmp, A(o_wn2[bi], D), AX,
                                                      apply_absolute_value=True),
                        lambda e, dst=dst: e.tensor_tensor(dst, dst, wtmp, Alu.max),
                    ], wait=[ev_l])
                    wn_ev[bi] = [ev]
            return task

        def t_w2row():
            def task():
                for doc in range(DC):
                    def trs2(e, doc=doc):
                        return e.transpose(ps_x[0:1, 0:P],
                                           sc[:, o_sw2 + doc:o_sw2 + doc + 1],
                                           A(o_idf, P))
                    ev_t = kb.step("tensor", trs2, wait=[wsh["sw2"], kb.mark("V")],
                                   sig="T")
                    def cps2(e, doc=doc):
                        return e.tensor_copy(
                            af_t[0:1, o_sw2row + doc * P:o_sw2row + (doc + 1) * P],
                            ps_x[0:1, 0:P])
                    kb.step("vector", cps2, wait=[ev_t], sig="V")
                def rrow2(e):
                    return e.reciprocal(af_t[0:1, o_rb2:o_rb2 + D],
                                        af_t[0:1, o_sw2row:o_sw2row + D])
                ev_rr = kb.step("vector", rrow2, wait=[kb.mark("V")], sig="V")
                wsh["bc"] = bcast_row(af_t[:, o_rb2:o_rb2 + D], A(o_wrbc, D), D,
                                      wait=[ev_rr])
            return task

        def t_wq(coff, bi, wt_src, s_dst, scat=None):
            def task():
                ev_l = kb.step("sync",
                               dma_to(A(o_wn2[bi], D), wt_src[:, coff:coff + D]),
                               wait=wn_ev[bi], sig="ld", sig_n=16)
                def q1(e, bi=bi):
                    return e.tensor_tensor(A(o_wn2[bi], D), A(o_wn2[bi], D),
                                           A(o_wrbc, D), Alu.mult)
                kb.step("vector", q1, wait=[ev_l, wsh["bc"]], sig="V")
                def q2(e, bi=bi):
                    return e.tensor_scalar(A(o_wn2[bi], D), A(o_wn2[bi], D),
                                           MAGIC, MAGIC, Alu.add, Alu.subtract)
                kb.step("vector", q2, sig="V")
                def q3(e, bi=bi):
                    return e.tensor_scalar(Ab(o_wcod2[bi], D), A(o_wn2[bi], D),
                                           -128.0, 127.0, Alu.max, Alu.min)
                ev_q = kb.step("vector", q3, wait=wcod_ev[bi], sig="V")
                wn_ev[bi] = [ev_q]
                if scat is None:
                    dst = s_dst[:, coff:coff + D]
                    srcc = Ab(o_wcod2[bi], D)
                else:
                    p_, dc_ = scat
                    dst = s_dst.rearrange("p (fc dd) -> p fc dd",
                                          dd=D)[:, p_ * JP:(p_ + 1) * JP,
                                                dc_ * P:(dc_ + 1) * P]
                    srcc = Ab(o_wcod2[bi], D).rearrange("p (j c) -> p j c", c=P)
                wcod_ev[bi] = [kb.step("scalar",
                                       dma_to(dst, srcc),
                                       wait=[ev_q], sig="sp", sig_n=16)]
            return task

        wni = [0]
        def nb2():
            b = wni[0] % 2
            wni[0] += 1
            return b

        for fcc in range(FC):
            W_TASKS.append(t_w1max(fcc, nb2()))
        W_TASKS.append(t_swfin("sw1", o_sw1, FC))
        for p_ in range(F // D):
            W_TASKS.append(t_w1row(p_))
            for dc in range(DC):
                W_TASKS.append(t_wq(dc * F + p_ * D, nb2(), w1t, s_w1c,
                                    scat=(p_, dc)))
        for doc in range(DC):
            for p_ in range(F // D):
                W_TASKS.append(t_w2max(doc, p_, nb2()))
        W_TASKS.append(t_swfin("sw2", o_sw2, DC))
        W_TASKS.append(t_w2row())
        for fcc in range(FC):
            W_TASKS.append(t_wq(fcc * D, nb2(), w2t, s_w2c))

        NPPN = 1 + sum(1 for mt_ in range(MT) if mt_ % 8 == 7 and mt_ < MT - 1)
        W_SLOTS = HPn * NCH * NPPN
        w_done = [0]
        w_slot = [0]

        def pump_w():
            w_slot[0] += 1
            target = (len(W_TASKS) * w_slot[0]) // W_SLOTS
            while w_done[0] < min(target, len(W_TASKS)):
                W_TASKS[w_done[0]]()
                w_done[0] += 1

        emit_loads(0, [evA_sp, evA_T, kb.mark("V"), kb.mark("S")])
        emit_cbias(0)

        ev_exp_prev = [None, None]
        ev_pv_prev = [None, None]
        for hp in range(HPn):
            b = hp % 2
            ev_lq, ev_lk, ev_lv = ld_ev[hp]
            ev_cb = cb_ev[hp]
            o_qhi, o_qlo = o_qhi2[b], o_qlo2[b]
            o_khi, o_klo = o_khi2[b], o_klo2[b]
            o_vtk = o_vtk2[b]
            negc_ap = sc[:, o_negc + b:o_negc + b + 1]

            for nch in range(NCH):
                ev_evac_prev = kb.mark("V")
                pend = None

                def emit_pv(mt, pbuf, ev_exp):
                    def pvm(e, mt=mt, pbuf=pbuf, o_vtk=o_vtk):
                        r = None
                        for head in range(2):
                            lhs = A(o_vtk + mt * 130 + head * 65, 65)
                            rhs = A(o_pb[pbuf] + head * 512, 512)
                            o = ps_pv[0:65, head * 512:head * 512 + 512]
                            r = e.matmul(o, lhs, rhs, start=(mt == 0), stop=(mt == MT - 1))
                        return r
                    w = [ev_exp, ev_lv]
                    if mt == 0:
                        w.append(ev_evac_prev)
                    ev = kb.step("tensor", pvm, wait=w, sig="T")
                    ev_pv_prev[pbuf] = ev
                    return ev

                for mt in range(MT):
                    pbuf = mt % 2
                    pspair = ps_a if pbuf == 0 else ps_b
                    def smm(e, nch=nch, mt=mt, pspair=pspair, o_qhi=o_qhi, o_qlo=o_qlo,
                            o_khi=o_khi, o_klo=o_klo):
                        r = None
                        for head in range(2):
                            qh_ = Ab(o_qhi + mt * P, P)[head * 64:head * 64 + 64, :]
                            ql_ = Ab(o_qlo + mt * P, P)[head * 64:head * 64 + 64, :]
                            kh_ = Ab(o_khi + nch * 512, 512)[head * 64:head * 64 + 64, :]
                            kl_ = Ab(o_klo + nch * 512, 512)[head * 64:head * 64 + 64, :]
                            o = pspair[:, head * 512:(head + 1) * 512]
                            tp = (head * 64, 0)
                            e.matmul(o, qh_, kh_, start=True, stop=False, tile_position=tp)
                            e.matmul(o, qh_, kl_, start=False, stop=False, tile_position=tp)
                            r = e.matmul(o, ql_, kh_, start=False, stop=True, tile_position=tp)
                        return r
                    w = [ev_lq, ev_lk, ev_exp_prev[pbuf]]
                    ev_sT = kb.step("tensor", smm, wait=w, sig="T")
                    def expf(e, pbuf=pbuf, pspair=pspair, negc_ap=negc_ap):
                        return e.activation(A(o_pb[pbuf], 1024), pspair[:, 0:1024], Act.Exp,
                                            bias=negc_ap, scale=1.0 / 32.0)
                    ev_exp = kb.step("scalar", expf,
                                     wait=[ev_sT, ev_pv_prev[pbuf], ev_cb], sig="S")
                    ev_exp_prev[pbuf] = ev_exp
                    if pend is not None:
                        emit_pv(*pend)
                    pend = (mt, pbuf, ev_exp)
                    if mt % 8 == 7 and mt < MT - 1:
                        pump_w()
                ev_pv = emit_pv(*pend)
                ev_bc_prev = None
                for head in range(2):
                    rbrow = o_rb + head * 512
                    rbsb = o_rbs + head * 512
                    def rcp2(e, head=head, rbrow=rbrow):
                        return e.reciprocal(af_t[64:65, rbrow:rbrow + 512],
                                            ps_pv[64:65, head * 512:head * 512 + 512])
                    ev_r = kb.step("vector", rcp2, wait=[ev_pv], sig="V")
                    def bcmm(e, rbrow=rbrow):
                        return e.matmul(ps_x[0:64, 0:512],
                                        af_t[64:65, o_ones64:o_ones64 + 64],
                                        af_t[64:65, rbrow:rbrow + 512],
                                        start=True, stop=True)
                    wbc = [ev_r, ev_ones, kb.mark("V")] if head == 0 else [ev_r, ev_bc_prev]
                    ev_bb = kb.step("tensor", bcmm, wait=wbc, sig="T")
                    def evbc(e, rbsb=rbsb):
                        return e.tensor_copy(A(rbsb, 512)[0:64, :], ps_x[0:64, 0:512])
                    ev_bc_prev = kb.step("vector", evbc, wait=[ev_bb], sig="V")
                    def onorm(e, head=head, nch=nch, rbsb=rbsb):
                        dst = A(o_ohp + head * NO + nch * 512, 512)[0:64, :]
                        return e.tensor_tensor(dst, ps_pv[0:64, head * 512:head * 512 + 512],
                                               A(rbsb, 512)[0:64, :], Alu.mult)
                    ev_on = kb.step("vector", onorm, wait=[ev_bc_prev], sig="V")
                pump_w()
                if nch == min(1, NCH - 1) and hp + 1 < HPn:
                    emit_loads(hp + 1, [kb.mark("T"), kb.mark("S")])
                    emit_cbias(hp + 1)
            def oh_(e):
                return e.tensor_copy(Ab(o_oh, 2 * NO)[0:64, :], A(o_ohp, 2 * NO)[0:64, :])
            kb.step("vector", oh_, wait=[kb.mark("sp")], sig="V")
            def ol_(e):
                return e.scalar_tensor_tensor(Ab(o_ol, 2 * NO)[0:64, :],
                                              A(o_ohp, 2 * NO)[0:64, :], 0.0,
                                              Ab(o_oh, 2 * NO)[0:64, :],
                                              Alu.add, Alu.subtract)
            ev_sp2 = kb.step("vector", ol_, sig="V")
            kb.step("scalar", dma_to(s_oh[:, hp * 2 * NO:(hp + 1) * 2 * NO],
                                     Ab(o_oh, 2 * NO)[0:64, :]),
                    wait=[ev_sp2], sig="sp", sig_n=16)
            kb.step("scalar", dma_to(s_ol[:, hp * 2 * NO:(hp + 1) * 2 * NO],
                                     Ab(o_ol, 2 * NO)[0:64, :]),
                    sig="sp", sig_n=16)

        while w_done[0] < len(W_TASKS):
            W_TASKS[w_done[0]]()
            w_done[0] += 1
        evB_T = kb.mark("T"); evB_sp = kb.mark("sp"); evB_V = kb.mark("V")
        evW_sp = kb.mark("sp")
        AF.pop(); AB.pop()

        AF.push(); AB.push()
        o_wst2 = AF.alloc(DC * D)
        o_woh = AB.alloc(DC * D); o_wol = AB.alloc(DC * D)
        ev_wo = split_weight(woT, o_woh, o_wol, None, o_wst2)
        o_obh = AB.alloc(HPn * NO)
        o_obl = AB.alloc(HPn * NO)
        ev_rolist = {}
        for hp in range(HPn):
            for j in range(2):
                def roh(e, hp=hp, j=j):
                    return e.dma_start(
                        out=ab_t[64 * j:64 * (j + 1), o_obh + hp * NO:o_obh + (hp + 1) * NO],
                        in_=s_oh[:, (2 * hp + j) * NO:(2 * hp + j + 1) * NO])
                w = [evB_sp, evB_T, evB_V, evW_sp] if (hp == 0 and j == 0) else []
                kb.step("sync", roh, wait=w, sig="ld", sig_n=16)
                def rol(e, hp=hp, j=j):
                    return e.dma_start(
                        out=ab_t[64 * j:64 * (j + 1), o_obl + hp * NO:o_obl + (hp + 1) * NO],
                        in_=s_ol[:, (2 * hp + j) * NO:(2 * hp + j + 1) * NO])
                ev_rolist[hp] = kb.step("sync", rol, sig="ld", sig_n=16)
        o_bo = AF.alloc(D)
        o_borow2 = AF.alloc(D)
        ev_lbo = kb.step("sync", dma_to(af_t[0:1, o_borow2:o_borow2 + D], borow),
                         sig="ld", sig_n=16)
        ev_bo = bcast_row(af_t[:, o_borow2:o_borow2 + D], A(o_bo, D), D,
                          wait=[ev_lbo])
        o_xt = AF.alloc(D)
        o_x2t = AF.alloc(D)
        o_sq2 = AF.alloc(D)
        o_lnsc = AF.alloc(D)
        def initlmax(e):
            return e.memset(sc[:, o_lmax:o_lmax + 1], -1e30)
        ev_lm0 = kb.step("vector", initlmax, sig="V")
        levacC = [None, None]
        gC = 0
        for tt in range(TT):
            ev_lx = kb.step("sync", dma_to(A(o_xt, D), xin[:, tt * D:(tt + 1) * D]),
                            wait=[kb.mark("V")], sig="ld", sig_n=16)
            def xpb(e):
                return e.tensor_tensor(A(o_xt, D), A(o_xt, D), A(o_bo, D), Alu.add)
            ev_xp = kb.step("vector", xpb, wait=[ev_lx, ev_bo], sig="V")
            for fo in range(NFO):
                half = gC % 2
                gC += 1
                pb = ps_a[:, half * 512:half * 512 + FOW]
                mlast = None
                for fc in range(DC):
                    ohh = Ab(o_obh + fc * NO + tt * P, P)
                    oll = Ab(o_obl + fc * NO + tt * P, P)
                    wh = Ab(o_woh + fc * D + fo * FOW, FOW)
                    wl = Ab(o_wol + fc * D + fo * FOW, FOW)
                    w = []
                    if fc == 0:
                        w = [ev_wo, kb.mark("V")] if gC == 1 else [levacC[half]]
                    w.append(ev_rolist[fc % HPn])
                    def mm(e, ohh=ohh, oll=oll, wh=wh, wl=wl, fc=fc, pb=pb):
                        e.matmul(pb, ohh, wh, start=(fc == 0), stop=False)
                        e.matmul(pb, ohh, wl, start=False, stop=False)
                        return e.matmul(pb, oll, wh, start=False, stop=(fc == DC - 1))
                    mlast = kb.step("tensor", mm, wait=w, sig="T")
                def evx2(e, fo=fo, pb=pb):
                    return e.tensor_tensor(A(o_x2t + fo * FOW, FOW), pb,
                                           A(o_xt + fo * FOW, FOW), Alu.add)
                levacC[half] = kb.step("vector", evx2,
                                       wait=[mlast, ev_xp, kb.mark("sp")], sig="V")
            def sq2f(e):
                return e.activation(A(o_sq2, D), A(o_x2t, D), Act.Square)
            ev_q2 = kb.step("scalar", sq2f, wait=[kb.mark("V")], sig="S")
            s1 = sc[:, o_stats:o_stats + 1]; s2 = sc[:, o_stats + 1:o_stats + 2]
            mu2c = sc[:, o_mu2 + tt:o_mu2 + tt + 1]
            r2c = sc[:, o_r2 + tt:o_r2 + tt + 1]
            ev_st2 = kb.chain("vector", [
                lambda e: (e.reduce_sum(s1, A(o_x2t, D), AX),
                           e.reduce_sum(s2, A(o_sq2, D), AX))[1],
                lambda e, mu2c=mu2c: (e.tensor_scalar(mu2c, s1, 1.0 / D, None, Alu.mult),
                                      e.tensor_scalar(s2, s2, 1.0 / D, None, Alu.mult))[1],
                lambda e, mu2c=mu2c: e.tensor_tensor(s1, mu2c, mu2c, Alu.mult),
                lambda e: e.tensor_tensor(s2, s2, s1, Alu.subtract),
                lambda e: e.tensor_scalar(s2, s2, 1e-5, None, Alu.add),
            ], wait=[ev_q2])
            def rt2(e):
                return e.activation(s2, s2, Act.Sqrt)
            ev_r2 = kb.step("scalar", rt2, wait=[ev_st2], sig="S")
            ev_f2 = kb.chain("vector", [
                lambda e, r2c=r2c: e.reciprocal(r2c, s2),
                lambda e, mu2c=mu2c, r2c=r2c: e.tensor_scalar(
                    A(o_lnsc, D), A(o_x2t, D), mu2c, r2c, Alu.subtract, Alu.mult),
                lambda e: e.reduce_max(s1, A(o_lnsc, D), AX),
                lambda e: e.tensor_tensor(sc[:, o_lmax:o_lmax + 1],
                                          sc[:, o_lmax:o_lmax + 1], s1, Alu.max),
            ], wait=[ev_r2, ev_lm0])
            kb.step("scalar", dma_to(s_x2[:, tt * D:(tt + 1) * D], A(o_x2t, D)),
                    wait=[ev_f2], sig="sp", sig_n=16)

        def trmax(e):
            return e.transpose(ps_x[0:1, 0:P], sc[:, o_lmax:o_lmax + 1], A(o_idf, P))
        ev_tm = kb.step("tensor", trmax, wait=[kb.mark("V")], sig="T")
        def redmax(e):
            return e.reduce_max(sc[0:1, o_lmax + 1:o_lmax + 2], ps_x[0:1, 0:P], AX)
        ev_rm = kb.step("vector", redmax, wait=[ev_tm], sig="V")
        ev_cc0 = kb.chain("vector", [
            lambda e: e.memset(af_t[0:1, o_ccb:o_ccb + 16], 0.0),
            lambda e: e.tensor_copy(af_t[0:1, o_ccb:o_ccb + 1],
                                    sc[0:1, o_lmax + 1:o_lmax + 2]),
        ], wait=[ev_rm])
        ev_cw = kb.step("gpsimd", dma_to(cc1i, af_t[0:1, o_ccb:o_ccb + 16]),
                        wait=[ev_cc0], sig="gd", sig_n=16)
        def ccrun(e):
            return e.collective_compute("AllReduce", Alu.max,
                                        replica_groups=[list(range(8))],
                                        ins=[cc1i], outs=[cc1o])
        ev_cc = kb.step("gpsimd", ccrun, wait=[ev_cw], sig="cc")
        ev_cr = kb.step("gpsimd", dma_to(af_t[0:1, o_ccb:o_ccb + 16], cc1o),
                        wait=[ev_cc], sig="gd", sig_n=16)
        ev_s1s = kb.chain("vector", [
            lambda e: e.tensor_scalar(sc[0:1, o_misc:o_misc + 1],
                                      af_t[0:1, o_ccb:o_ccb + 1],
                                      1e-8, 1.0 / 255.0, Alu.max, Alu.mult),
            lambda e: e.reciprocal(sc[0:1, o_misc + 1:o_misc + 2],
                                   sc[0:1, o_misc:o_misc + 1]),
        ], wait=[ev_cr])
        bcast_row(sc[:, o_misc:o_misc + 1], sc[:, o_misc + 2:o_misc + 3], 1,
                  wait=[ev_s1s])
        ev_bs1 = bcast_row(sc[:, o_misc + 1:o_misc + 2], sc[:, o_misc + 3:o_misc + 4], 1,
                           wait=[ev_s1s])
        ev_sc1 = kb.chain("vector", [
            lambda e: e.tensor_scalar(sc[:, o_sc1:o_sc1 + FC], sc[:, o_sw1:o_sw1 + FC],
                                      sc[:, o_misc + 2:o_misc + 3], None, Alu.mult),
            lambda e: e.reciprocal(sc[:, o_stats:o_stats + FC], sc[:, o_sc1:o_sc1 + FC]),
            lambda e: e.tensor_tensor(sc[:, o_b1q:o_b1q + FC], sc[:, o_b1q:o_b1q + FC],
                                      sc[:, o_stats:o_stats + FC], Alu.mult),
            lambda e: e.tensor_scalar(sc[:, o_b1q:o_b1q + FC], sc[:, o_b1q:o_b1q + FC],
                                      MAGIC, MAGIC, Alu.add, Alu.subtract),
            lambda e: e.tensor_tensor(sc[:, o_b1q:o_b1q + FC], sc[:, o_b1q:o_b1q + FC],
                                      sc[:, o_sc1:o_sc1 + FC], Alu.mult),
        ], wait=[ev_bs1, wsh["sw1"], ev_b1c])
        evC_sp = kb.mark("sp"); evC_T = kb.mark("T")
        AF.pop(); AB.pop()

        AF.push(); AB.push()
        o_x2t2 = AF.alloc(D)
        o_lnq = AF.alloc(D)
        o_xqT = AB.alloc(DC * NO)
        o_ctk = AB.alloc(D)
        ev_evcD = None
        for tt in range(TT):
            ev_l = kb.step("sync", dma_to(A(o_x2t2, D), s_x2[:, tt * D:(tt + 1) * D]),
                           wait=[evC_sp, evC_T, kb.mark("V")] if tt == 0 else [kb.mark("V")],
                           sig="ld", sig_n=16)
            def qchain(e, tt=tt):
                e.tensor_scalar(A(o_lnq, D), A(o_x2t2, D),
                                sc[:, o_mu2 + tt:o_mu2 + tt + 1],
                                sc[:, o_r2 + tt:o_r2 + tt + 1], Alu.subtract, Alu.mult)
                e.tensor_scalar(A(o_lnq, D), A(o_lnq, D),
                                sc[:, o_misc + 3:o_misc + 4], MAGIC, Alu.mult, Alu.add)
                e.tensor_scalar(A(o_lnq, D), A(o_lnq, D), MAGIC, 0.0,
                                Alu.subtract, Alu.max)
                return e.tensor_scalar(Ab(o_ctk, D), A(o_lnq, D), 255.0, None, Alu.min)
            ev_q = kb.step("vector", qchain, wait=[ev_l, ev_sc1], sig="V")
            ev_t = None
            for dc in range(DC):
                def trc(e, dc=dc):
                    return e.transpose(ps_xb[:, dc * P:(dc + 1) * P],
                                       Ab(o_ctk + dc * P, P), Ab(o_idb, P))
                w = [ev_q] + ([ev_evcD] if dc == 0 else [])
                ev_t = kb.step("tensor", trc, wait=w, sig="T")
            def evc(e, tt=tt):
                dst = ab_t[:, o_xqT:o_xqT + DC * NO].rearrange(
                    "p (dc no) -> p dc no", dc=DC)[:, :, tt * P:(tt + 1) * P]
                return e.tensor_copy(dst, ps_xb[:, 0:DC * P].rearrange(
                    "p (dc c) -> p dc c", dc=DC))
            ev_evcD = kb.step("vector", evc, wait=[ev_t], sig="V")
        evD_V = kb.mark("V")

        o_w1ct = AB.alloc(DC * F)
        ev_rw1 = kb.step("sync", dma_to(Ab(o_w1ct, DC * F), s_w1c[:, :]),
                         wait=[evW_sp, evC_T], sig="ld", sig_n=16)
        o_hrow = AF.alloc(NO)
        def inith(e):
            return e.memset(sc[:, o_lmax + 2:o_lmax + 3], 0.0)
        ev_h0 = kb.step("vector", inith, sig="V")
        ev_sph_prev = kb.mark("sp")
        ev_w2ld = {}
        levacE = [None, None]
        gE = 0
        for fcc in range(FC):
            for tch in range(NO // 512):
                half = gE % 2
                gE += 1
                pb = ps_a[:, half * 512:half * 512 + 512]
                mlast = None
                for dc in range(DC):
                    lhs = Ab(o_w1ct + fcc * D + dc * P, P)
                    rhs = Ab(o_xqT + dc * NO + tch * 512, 512)
                    w = []
                    if dc == 0:
                        w = [ev_rw1, evD_V] if gE == 1 else [levacE[half]]
                    def mm(e, lhs=lhs, rhs=rhs, dc=dc, pb=pb):
                        return e.matmul(pb, lhs, rhs, start=(dc == 0),
                                        stop=(dc == DC - 1))
                    mlast = kb.step("tensor", mm, wait=w, sig="T")
                def evh(e, fcc=fcc, tch=tch, pb=pb):
                    return e.activation(A(o_hrow + tch * 512, 512), pb,
                                        Act.Relu,
                                        bias=sc[:, o_b1q + fcc:o_b1q + fcc + 1],
                                        scale=sc[:, o_sc1 + fcc:o_sc1 + fcc + 1])
                levacE[half] = kb.step("scalar", evh,
                        wait=[mlast, ev_sc1, ev_sph_prev, kb.mark("V")], sig="S")
            ev_hv = kb.mark("S")
            kb.chain("vector", [
                lambda e: e.reduce_max(sc[:, o_stats:o_stats + 1], A(o_hrow, NO), AX),
                lambda e: e.tensor_tensor(sc[:, o_lmax + 2:o_lmax + 3],
                                          sc[:, o_lmax + 2:o_lmax + 3],
                                          sc[:, o_stats:o_stats + 1], Alu.max),
            ], wait=[ev_hv, ev_h0])
            ev_sph_prev = kb.step("scalar", dma_to(s_h[:, fcc * NO:(fcc + 1) * NO],
                                                   A(o_hrow, NO)),
                                  sig="sp", sig_n=16)
            ev_w2ld[fcc] = kb.step("sync",
                                   dma_to(Ab(o_w1ct + fcc * D, D),
                                          s_w2c[:, fcc * D:(fcc + 1) * D]),
                                   wait=[mlast, evW_sp], sig="ld", sig_n=16)

        def trmax2(e):
            return e.transpose(ps_x[0:1, 0:P], sc[:, o_lmax + 2:o_lmax + 3], A(o_idf, P))
        ev_tm2 = kb.step("tensor", trmax2, wait=[kb.mark("V")], sig="T")
        def redmax2(e):
            return e.reduce_max(sc[0:1, o_lmax + 3:o_lmax + 4], ps_x[0:1, 0:P], AX)
        ev_rm2 = kb.step("vector", redmax2, wait=[ev_tm2], sig="V")
        ev_cc20 = kb.chain("vector", [
            lambda e: e.memset(af_t[0:1, o_ccb:o_ccb + 16], 0.0),
            lambda e: e.tensor_copy(af_t[0:1, o_ccb:o_ccb + 1],
                                    sc[0:1, o_lmax + 3:o_lmax + 4]),
        ], wait=[ev_rm2])
        ev_cw2 = kb.step("gpsimd", dma_to(cc2i, af_t[0:1, o_ccb:o_ccb + 16]),
                         wait=[ev_cc20], sig="gd", sig_n=16)
        def ccrun2(e):
            return e.collective_compute("AllReduce", Alu.max,
                                        replica_groups=[list(range(8))],
                                        ins=[cc2i], outs=[cc2o])
        ev_cc2 = kb.step("gpsimd", ccrun2, wait=[ev_cw2], sig="cc")
        ev_cr2 = kb.step("gpsimd", dma_to(af_t[0:1, o_ccb:o_ccb + 16], cc2o),
                         wait=[ev_cc2], sig="gd", sig_n=16)
        ev_s2s = kb.chain("vector", [
            lambda e: e.tensor_scalar(sc[0:1, o_misc + 8:o_misc + 9],
                                      af_t[0:1, o_ccb:o_ccb + 1],
                                      1e-8, 1.0 / 255.0, Alu.max, Alu.mult),
            lambda e: e.reciprocal(sc[0:1, o_misc + 9:o_misc + 10],
                                   sc[0:1, o_misc + 8:o_misc + 9]),
        ], wait=[ev_cr2])
        ev_bs2 = bcast_row(sc[:, o_misc + 9:o_misc + 10], sc[:, o_misc + 10:o_misc + 11],
                           1, wait=[ev_s2s])
        o_s2sb = AF.alloc(D)
        o_b2qb = AF.alloc(D)
        ev_mr2 = kb.chain("vector", [
            lambda e: e.tensor_scalar(af_t[0:1, o_sw2row:o_sw2row + D],
                                      af_t[0:1, o_sw2row:o_sw2row + D],
                                      sc[0:1, o_misc + 8:o_misc + 9], None, Alu.mult),
            lambda e: e.reciprocal(af_t[0:1, o_rb2:o_rb2 + D],
                                   af_t[0:1, o_sw2row:o_sw2row + D]),
        ], wait=[ev_s2s])
        ev_lb2 = kb.step("sync", dma_to(af_t[0:1, o_b2r:o_b2r + D], b2row),
                         sig="ld", sig_n=16)
        ev_b2q = kb.chain("vector", [
            lambda e: e.tensor_tensor(af_t[0:1, o_b2r:o_b2r + D],
                                      af_t[0:1, o_b2r:o_b2r + D],
                                      af_t[0:1, o_rb2:o_rb2 + D], Alu.mult),
            lambda e: e.tensor_scalar(af_t[0:1, o_b2r:o_b2r + D],
                                      af_t[0:1, o_b2r:o_b2r + D],
                                      MAGIC, MAGIC, Alu.add, Alu.subtract),
            lambda e: e.tensor_tensor(af_t[0:1, o_b2r:o_b2r + D],
                                      af_t[0:1, o_b2r:o_b2r + D],
                                      af_t[0:1, o_sw2row:o_sw2row + D], Alu.mult),
        ], wait=[ev_mr2, ev_lb2])
        bcast_row(af_t[:, o_sw2row:o_sw2row + D], A(o_s2sb, D), D, wait=[ev_b2q])
        ev_bb2 = bcast_row(af_t[:, o_b2r:o_b2r + D], A(o_b2qb, D), D, wait=[ev_b2q])

        o_hf2 = [AF.alloc(FC * 256) for _ in range(2)]
        if 2 * FC * 256 <= DC * NO:
            o_hq2 = [o_xqT, o_xqT + FC * 256]
        else:
            o_hq2 = [o_xqT, AB.alloc(FC * 256)]
        o_ot = AF.alloc(D)
        NPART = 4
        PFC = FC // NPART
        HSEG = FC * 256 // NPART
        ev_ldq = {}
        ev_qp = {}
        ev_mml = {}
        ev_qlast = {}

        def emit_hload(ts, part):
            b = ts % 2
            f0, f1 = part * PFC, (part + 1) * PFC
            def rdh(e, ts=ts, f0=f0, f1=f1, b=b):
                src2 = s_h.rearrange("p (fc no) -> p fc no", fc=FC)[:, f0:f1,
                                                                   ts * 256:(ts + 1) * 256]
                dst = A(o_hf2[b], FC * 256).rearrange("p (fc c) -> p fc c",
                                                      fc=FC)[:, f0:f1, :]
                return e.dma_start(out=dst, in_=src2)
            w = []
            if part == 0:
                w = ([kb.mark("sp"), kb.mark("V"), kb.mark("T")] if ts < 2
                     else [ev_qlast[ts - 2]])
            ev_ldq[(ts, part)] = kb.step("sync", rdh, wait=w, sig="ld", sig_n=16)

        def emit_hq(ts, part):
            b = ts % 2
            o0 = part * HSEG
            def qp(e, b=b, o0=o0):
                e.tensor_scalar(af_t[:, o_hf2[b] + o0:o_hf2[b] + o0 + HSEG],
                                af_t[:, o_hf2[b] + o0:o_hf2[b] + o0 + HSEG],
                                sc[:, o_misc + 10:o_misc + 11], MAGIC,
                                Alu.mult, Alu.add)
                return e.tensor_scalar(ab_t[:, o_hq2[b] + o0:o_hq2[b] + o0 + HSEG],
                                       af_t[:, o_hf2[b] + o0:o_hf2[b] + o0 + HSEG],
                                       MAGIC, 255.0, Alu.subtract, Alu.min)
            w = [ev_ldq[(ts, part)], ev_bs2]
            if part == 0:
                w.append(kb.mark("T") if ts < 2 else ev_mml[ts - 2])
            ev_qp[(ts, part)] = kb.step("vector", qp, wait=w, sig="V")
            if part == NPART - 1:
                ev_qlast[ts] = ev_qp[(ts, part)]

        gF = [0]
        levacF = [None, None]
        for part in range(NPART):
            emit_hload(0, part)
            emit_hq(0, part)
        for ts in range(TS2):
            b = ts % 2
            nxt = [] if ts + 1 >= TS2 else [(ts + 1, p) for p in range(NPART)]
            pf = [0]
            def pump_f(nxt=nxt, pf=pf):
                if pf[0] < len(nxt):
                    t2, p2 = nxt[pf[0]]
                    emit_hload(t2, p2)
                    emit_hq(t2, p2)
                    pf[0] += 1
            mlastF = None
            for tt in range(2):
                gt = ts * 2 + tt
                ev_lx2 = kb.step("sync", dma_to(A(o_x2t2, D), s_x2[:, gt * D:(gt + 1) * D]),
                                 wait=[kb.mark("V")], sig="ld", sig_n=16)
                def x2pb(e):
                    return e.tensor_tensor(A(o_x2t2, D), A(o_x2t2, D), A(o_b2qb, D), Alu.add)
                ev_xb = kb.step("vector", x2pb, wait=[ev_lx2, ev_bb2], sig="V")
                for doc in range(NFO):
                    half = gF[0] % 2
                    gF[0] += 1
                    pb = ps_a[:, half * 512:half * 512 + FOW]
                    for fcc in range(FC):
                        lhs = Ab(o_hq2[b] + fcc * 256 + tt * P, P)
                        rhs = Ab(o_w1ct + fcc * D + doc * FOW, FOW)
                        w = []
                        if fcc == 0:
                            w = [ev_qp[(ts, 0)]]
                            w.append(levacF[half] if gF[0] > 2 else kb.mark("V"))
                        elif fcc % PFC == 0:
                            w = [ev_qp[(ts, fcc // PFC)]]
                        if ts == 0:
                            w.append(ev_w2ld[fcc])
                        def mm(e, lhs=lhs, rhs=rhs, fcc=fcc, pb=pb):
                            return e.matmul(pb, lhs, rhs, start=(fcc == 0),
                                            stop=(fcc == FC - 1))
                        mlastF = kb.step("tensor", mm, wait=w, sig="T")
                    def evo(e, doc=doc, pb=pb):
                        e.tensor_tensor(A(o_ot + doc * FOW, FOW), pb,
                                        A(o_s2sb + doc * FOW, FOW), Alu.mult)
                        return e.tensor_tensor(A(o_ot + doc * FOW, FOW),
                                               A(o_ot + doc * FOW, FOW),
                                               A(o_x2t2 + doc * FOW, FOW), Alu.add)
                    levacF[half] = kb.step("vector", evo,
                                           wait=[mlastF, ev_xb, kb.mark("sp")], sig="V")
                    pump_f()
                ev_ot = kb.mark("V")
                kb.step("scalar", dma_to(out[:, gt * D:(gt + 1) * D], A(o_ot, D)),
                        wait=[ev_ot], sig="sp", sig_n=16)
            while pf[0] < len(nxt):
                pump_f()
            ev_mml[ts] = mlastF

        if dbg:
            dbg_list = [("d_qh", s_qh), ("d_ql", s_ql), ("d_kh", s_kh),
                        ("d_vt", s_vt), ("d_oh", s_oh), ("d_ol", s_ol),
                        ("d_x2", s_x2), ("d_h", s_h), ("d_w1c", s_w1c), ("d_w2c", s_w2c),
                        ("d_l1h", s_l1h), ("d_l1l", s_l1l)]
            dbg_list += [("d_cc1", cc1o), ("d_cc2", cc2o)]
            wall = [kb.mark("sp"), kb.mark("ld"), kb.mark("V"), kb.mark("S"),
                    kb.mark("T")]
            for nm, src in dbg_list:
                t = src.tensor
                d = nc.dram_tensor(nm, list(t.shape), t.dtype, kind="ExternalOutput").ap()
                kb.step("gpsimd", dma_to(d, src), wait=wall, sig="gd", sig_n=16)
            d_sc = nc.dram_tensor("d_sc", [P, 768], F32, kind="ExternalOutput").ap()
            kb.step("gpsimd", dma_to(d_sc, sc[:, 0:768]), wait=wall, sig="gd", sig_n=16)

        final_sp = kb.mark("sp")
        final_gd = kb.mark("gd")
        kb.step("gpsimd", lambda e: None, wait=[final_sp, final_gd])

        with nc.Block() as block:
            kb.emit(block)

    return nc


def shard_inputs(ins, cfg):
    D, NB, H, F = cfg["D"], cfg["NB"], cfg["H"], cfg["F"]
    DC, FC, MT = D // P, F // P, NB // P
    x = np.ascontiguousarray(ins["L0_attn_x"], dtype=np.float32)
    B_ = x.shape[0]

    def colmajor(w):
        Aq, B2 = w.shape
        return np.ascontiguousarray(
            w.reshape(Aq // P, P, B2).transpose(1, 0, 2).reshape(P, (Aq // P) * B2))

    common = dict(
        wqT=colmajor(np.ascontiguousarray(np.asarray(ins["L0_attn_Wq"], np.float32).T)),
        wkT=colmajor(np.ascontiguousarray(np.asarray(ins["L0_attn_Wk"], np.float32).T)),
        wvT=colmajor(np.ascontiguousarray(np.asarray(ins["L0_attn_Wv"], np.float32).T)),
        woT=colmajor(np.ascontiguousarray(np.asarray(ins["L0_attn_Wo"], np.float32).T)),
        w1n=colmajor(np.asarray(ins["L0_ffn_w1"], np.float32)),
        w1t=colmajor(np.ascontiguousarray(np.asarray(ins["L0_ffn_w1"], np.float32).T)),
        w2n=colmajor(np.asarray(ins["L0_ffn_w2"], np.float32)),
        w2t=colmajor(np.ascontiguousarray(np.asarray(ins["L0_ffn_w2"], np.float32).T)),
        bqc=np.ascontiguousarray(np.asarray(ins["L0_attn_bq"], np.float32).reshape(DC, P).T),
        bkc=np.ascontiguousarray(np.asarray(ins["L0_attn_bk"], np.float32).reshape(DC, P).T),
        bvc=np.ascontiguousarray(np.asarray(ins["L0_attn_bv"], np.float32).reshape(DC, P).T),
        borow=np.ascontiguousarray(np.asarray(ins["L0_attn_bo"], np.float32).reshape(1, D)),
        b1c=np.ascontiguousarray(np.asarray(ins["L0_ffn_b1"], np.float32).reshape(FC, P).T),
        b2row=np.ascontiguousarray(np.asarray(ins["L0_ffn_b2"], np.float32).reshape(1, D)),
    )
    in_maps = []
    for c in range(8):
        b, h = c // 2, c % 2
        xb = x[b]
        xr = np.concatenate([xb[h * (NB // 2):(h + 1) * (NB // 2)],
                             xb[(1 - h) * (NB // 2):(2 - h) * (NB // 2)]], axis=0)
        xin = np.ascontiguousarray(
            xr.reshape(MT, P, D).transpose(1, 0, 2).reshape(P, MT * D))
        m = dict(common)
        m["xin"] = xin
        in_maps.append(m)
    return in_maps


def kernel(**inputs):
    cfg = FULL_CFG
    D, NB = cfg["D"], cfg["NB"]
    NO, TT = NB // 2, NB // 2 // P
    nc = build_nc(cfg)
    in_maps = shard_inputs(inputs, cfg)
    res = run_bass_kernel_spmd(nc, in_maps, core_ids=list(range(8)))
    B_ = inputs["L0_attn_x"].shape[0]
    outv = np.zeros((B_, NB, D), dtype=np.float32)
    for c in range(8):
        b, h = c // 2, c % 2
        o = np.asarray(res.results[c]["out"])
        o = o.reshape(P, TT, D).transpose(1, 0, 2).reshape(NO, D)
        outv[b, h * NO:(h + 1) * NO] = o
    return outv
